# version 25
# baseline (speedup 1.0000x reference)
import contextlib
import math
import numpy as np
import ml_dtypes
import concourse.bass as bass
import concourse.mybir as mybir
from concourse.bass_utils import run_bass_kernel_spmd
from concourse.bass import IndirectOffsetOnAxis

F32 = mybir.dt.float32
BF16 = mybir.dt.bfloat16
I32 = mybir.dt.int32
AF = mybir.ActivationFunctionType
ALU = mybir.AluOpType
BF = ml_dtypes.bfloat16

NCORES = 8
T = 1024
TC = 512
NCH = T // TC
DM = 4096
DMT = DM // 128
S_KEYS = 4096
KTOT = S_KEYS // 128
RUP = 1408
HT = 86
DNR = 512

NF4 = np.array([-1.0, -0.6961928009986877, -0.5250730514526367, -0.39491748809814453,
                -0.28444138169288635, -0.18477343022823334, -0.09105003625154495, 0.0,
                0.07958029955625534, 0.16093020141124725, 0.24611230194568634,
                0.33791524171829224, 0.44070982933044434, 0.5626170039176941,
                0.7229568362236023, 1.0], dtype=np.float32)
NF4_D = [float(np.float64(NF4[k]) - np.float64(NF4[k - 1])) for k in range(1, 16)]

EPS = 1e-5
MASKVAL = -10000.0


def _consts_np():
    hi = NF4.astype(BF)
    lo = (NF4.astype(np.float64) - hi.astype(np.float64)).astype(BF)
    dqd = np.zeros((16, 128, 256), dtype=BF)
    for k in range(16):
        dqd[k, :, 0:128] = np.eye(128, dtype=np.float32) * np.float32(hi[k])
        dqd[k, :, 128:256] = np.eye(128, dtype=np.float32) * np.float32(lo[k])
    return dict(
        c_ident=np.eye(128, dtype=BF),
        c_ones_bf=np.ones((128, 1), dtype=BF),
        c_ones_bfr=np.ones((1, 128), dtype=BF),
        c_ones_f=np.ones((1, 128), dtype=np.float32),
        c_dqd=dqd.reshape(16 * 128, 256),
    )


class _Cn:
    def __init__(self, h):
        self.h = h
        self.n = 0

    def inc(self, instr, k=1):
        instr.then_inc(self.h, k)
        self.n += k
        return instr


def _sems(nc, stack, names):
    return {nm: _Cn(stack.enter_context(nc.semaphore(nm))) for nm in names}


class DmaStream:

    def __init__(self, nc, stack, name, nslots):
        self.sems = [_Cn(stack.enter_context(nc.semaphore(f"{name}{k}")))
                     for k in range(nslots)]
        self.K = nslots
        self.i = 0

    def issue(self, eng, instr_fn):
        k = self.i % self.K
        if self.i >= self.K:
            eng.wait_ge(self.sems[k].h, 16 * (self.i // self.K))
        self.sems[k].inc(instr_fn(), 16)
        self.i += 1
        return self.i - 1

    def wait(self, eng, i):
        eng.wait_ge(self.sems[i % self.K].h, 16 * (i // self.K + 1))

    def wait_all(self, eng):
        for c in self.sems:
            if c.n:
                eng.wait_ge(c.h, c.n)


def _fm(ap):
    return ap.rearrange("(t p) n -> p t n", p=128)


SEM_NAMES = [
    "chV", "sCC",
    "sx", "ssq", "smm", "sE", "sTr", "sPk", "sN", "sNv", "sO", "sBr", "sQr",
    "sq", "skn", "sKm", "sMr", "sSc", "sEv", "sEx", "sSm", "sMb", "sRc", "sRb",
    "sRbc", "sAv", "sAe", "sPo", "sOe", "sOa", "sMk", "sMr2", "sCp1", "sLn",
    "sM1", "sBc1", "sI", "sDq", "sL", "sLl", "sU1", "sU1e", "sUp", "sUe",
    "sU2", "sU2e", "sDn", "sDb", "sDe", "sPc", "sBc", "sBe", "sPt", "sXp",
    "sXe", "sGe", "sYe", "sPt2", "sYn",
]


def _mkstreams(nc, st):
    return dict(
        A2=DmaStream(nc, st, "stA", 2), B2=DmaStream(nc, st, "stB", 2),
        C2=DmaStream(nc, st, "stC", 2), D2=DmaStream(nc, st, "stD", 2),
        E4=DmaStream(nc, st, "stE", 4), F3=DmaStream(nc, st, "stF", 3),
        G3=DmaStream(nc, st, "stG", 3), H1=DmaStream(nc, st, "stH", 1),
    )



def emit_pre(nc, st, s, sd, csb, io, layer, tag):
    if layer == 0:
        F_OUT, NQ, NK, NV, QSCALE, NHEAD = 24, 8, 8, 8, 1.0 / 32.0, 1
    else:
        F_OUT, NQ, NK, NV, QSCALE, NHEAD = 48, 16, 16, 16, 1.0, 8
    xin = io["xin"]
    wtile = io["wtile"]
    bvec = io["bvec"]
    o_xn, o_q, o_k, o_v = io["o_xn"], io["o_q"], io["o_k"], io["o_v"]
    o_qn, o_kn = io["o_qn"], io["o_kn"]
    v_fm = io["v_fm"]
    ag_mid = io.get("ag_mid")
    ag_early = io.get("ag_early")
    xin_nocast = io.get("xin_nocast", False)
    pre_gate = io.get("pre_gate")
    ones_bf, ones_f, ident = csb["ones_bf"], csb["ones_f"], csb["ident"]
    wst, vost, vlst, ovst, nst = sd["A2"], sd["B2"], sd["C2"], sd["D2"], sd["E4"]
    wstb = vlst

    with contextlib.ExitStack() as pst:
        E = pst.enter_context
        xn = E(nc.sbuf_tensor(f"{tag}xn", [128, DMT, T], BF16))
        q_sb = E(nc.sbuf_tensor(f"{tag}q_sb", [128, NQ, T], BF16))
        k_sb = E(nc.sbuf_tensor(f"{tag}k_sb", [128, NK, T], BF16))
        bias_sb = E(nc.sbuf_tensor(f"{tag}bias", [128, F_OUT], F32))
        sqb = E(nc.sbuf_tensor(f"{tag}sqb", [128, 2, TC], BF16))
        scr = E(nc.sbuf_tensor(f"{tag}scr", [128, TC], F32))
        srow = E(nc.sbuf_tensor(f"{tag}srow", [1, NCH, TC], F32))
        vrow = E(nc.sbuf_tensor(f"{tag}vrow", [1, NCH, TC], F32))
        m2row = E(nc.sbuf_tensor(f"{tag}m2row", [1, TC], F32))
        mb_sb = E(nc.sbuf_tensor(f"{tag}mb", [128, NCH, TC], F32))
        rb_sb = E(nc.sbuf_tensor(f"{tag}rb", [128, NCH, TC], F32))
        wbuf = E(nc.sbuf_tensor(f"{tag}wbuf", [128, 2, DMT, 128], BF16))
        vscr = E(nc.sbuf_tensor(f"{tag}vscr", [128, 2, TC], BF16))
        vld = E(nc.sbuf_tensor(f"{tag}vld", [128, 2, TC], BF16))
        vstg = E(nc.sbuf_tensor(f"{tag}vstg", [128, 2, 4, 128], BF16))
        nscr = E(nc.sbuf_tensor(f"{tag}nscr", [128, 2, T], BF16))
        nslot = E(nc.sbuf_tensor(f"{tag}nslot", [1, 4, TC], F32))

        sx0 = s["sx"].n
        so_bias0 = s["sO"].n

        with nc.Block() as block:
            @block.sync
            def _(sy):
                for f in range(F_OUT):
                    s["sO"].inc(sy.dma_start(out=bias_sb[:, f:f + 1],
                                             in_=bvec[f * 128:(f + 1) * 128, 0:1]), 16)
                sy.wait_ge(s["sO"].h, s["sO"].n)

            @block.gpsimd
            def _(g):
                if pre_gate is not None:
                    g.wait_ge(s["skn"].h, pre_gate)
                s["sx"].inc(g.dma_start(out=xn[:, :, :], in_=_fm(xin)), 16)
                g.wait_ge(s["sx"].h, sx0 + 16)

        if QSCALE != 1.0:
            with nc.Block() as block:
                @block.vector
                def _(v):
                    v.tensor_scalar(bias_sb[:, 0:NQ], bias_sb[:, 0:NQ], QSCALE, None, ALU.mult)

        with nc.Block() as block, \
                nc.psum_tensor(f"{tag}ps_s0", [1, TC], F32) as ps_s0, \
                nc.psum_tensor(f"{tag}ps_s1", [1, TC], F32) as ps_s1, \
                nc.psum_tensor(f"{tag}ps_q0", [1, TC], F32) as ps_q0, \
                nc.psum_tensor(f"{tag}ps_q1", [1, TC], F32) as ps_q1:
            ps_sum = [ps_s0, ps_s1]
            ps_ssq = [ps_q0, ps_q1]
            ssq0 = s["ssq"].n
            smm0 = s["smm"].n

            @block.scalar
            def _(sc):
                for c in range(NCH):
                    for kt in range(DMT):
                        i = c * DMT + kt
                        if i >= 2:
                            sc.wait_ge(s["smm"].h, smm0 + i - 1)
                        s["ssq"].inc(sc.activation(sqb[:, i % 2, :],
                                                   xn[:, kt, c * TC:(c + 1) * TC], AF.Square))

            @block.tensor
            def _(t):
                for c in range(NCH):
                    for kt in range(DMT):
                        i = c * DMT + kt
                        t.matmul(ps_sum[c][:, :], ones_bf[:, :], xn[:, kt, c * TC:(c + 1) * TC],
                                 start=(kt == 0), stop=(kt == DMT - 1))
                        t.wait_ge(s["ssq"].h, ssq0 + i + 1)
                        s["smm"].inc(t.matmul(ps_ssq[c][:, :], ones_bf[:, :], sqb[:, i % 2, :],
                                              start=(kt == 0), stop=(kt == DMT - 1)))

            @block.vector
            def _(v):
                ch = s["chV"]
                for c in range(NCH):
                    v.wait_ge(s["smm"].h, smm0 + DMT * (c + 1))
                    ch.inc(v.tensor_scalar(srow[:, c, :], ps_sum[c][:, :], 1.0 / DM, None, ALU.mult))
                    ch.inc(v.tensor_scalar(vrow[:, c, :], ps_ssq[c][:, :], 1.0 / DM, EPS, ALU.mult, ALU.add))
                    v.wait_ge(ch.h, ch.n)
                    ch.inc(v.tensor_mul(m2row[:, :], srow[:, c, :], srow[:, c, :]))
                    v.wait_ge(ch.h, ch.n)
                    ch.inc(v.tensor_sub(vrow[:, c, :], vrow[:, c, :], m2row[:, :]))

        with nc.Block() as block:
            @block.scalar
            def _(sc):
                sc.activation(vrow[:, :, :], vrow[:, :, :], AF.Sqrt)
        with nc.Block() as block:
            @block.vector
            def _(v):
                v.reciprocal(vrow[:, :, :], vrow[:, :, :])

        with nc.Block() as block, \
                nc.psum_tensor(f"{tag}ps_m0", [128, TC], F32) as ps_m0, \
                nc.psum_tensor(f"{tag}ps_m1", [128, TC], F32) as ps_m1, \
                nc.psum_tensor(f"{tag}ps_r0", [128, TC], F32) as ps_r0, \
                nc.psum_tensor(f"{tag}ps_r1", [128, TC], F32) as ps_r1:
            ps_m = [ps_m0, ps_m1]
            ps_r = [ps_r0, ps_r1]
            br0 = s["sBr"].n
            se0 = s["sE"].n
            sn0 = s["sN"].n

            @block.tensor
            def _(t):
                for c in range(NCH):
                    t.matmul(ps_m[c][:, :], ones_f[:, :], srow[:, c, :], start=True, stop=True)
                    s["sBr"].inc(t.matmul(ps_r[c][:, :], ones_f[:, :], vrow[:, c, :],
                                          start=True, stop=True))

            @block.scalar
            def _(sc):
                for c in range(NCH):
                    sc.wait_ge(s["sBr"].h, br0 + c + 1)
                    sc.activation(mb_sb[:, c, :], ps_m[c][:, :], AF.Identity)
                    s["sE"].inc(sc.activation(rb_sb[:, c, :], ps_r[c][:, :], AF.Identity))

            @block.vector
            def _(v):
                ch = s["chV"]
                for c in range(NCH):
                    v.wait_ge(s["sE"].h, se0 + c + 1)
                    for kt in range(DMT):
                        sl = xn[:, kt, c * TC:(c + 1) * TC]
                        if s["sN"].n > sn0:
                            v.wait_ge(s["sN"].h, s["sN"].n)
                        ch.inc(v.tensor_sub(scr[:, :], sl, mb_sb[:, c, :]))
                        v.wait_ge(ch.h, ch.n)
                        s["sN"].inc(v.tensor_mul(sl, scr[:, :], rb_sb[:, c, :]))

            @block.sync
            def _(sy):
                sy.wait_ge(s["sN"].h, sn0 + NCH * DMT)
                s["sO"].inc(sy.dma_start(out=_fm(o_xn), in_=xn[:, :, :]), 16)

        with nc.Block() as block, \
                nc.psum_tensor(f"{tag}ps_a", [128, TC], F32) as ps_a, \
                nc.psum_tensor(f"{tag}ps_b", [128, TC], F32) as ps_b, \
                nc.psum_tensor(f"{tag}ps_c", [128, TC], F32) as ps_c, \
                nc.psum_tensor(f"{tag}ps_d", [128, TC], F32) as ps_d, \
                nc.psum_tensor(f"{tag}ps_t0", [128, 512], BF16) as ps_t0, \
                nc.psum_tensor(f"{tag}ps_t1", [128, 512], BF16) as ps_t1:
            ps_main = [ps_a, ps_b, ps_c, ps_d]
            ps_tr = [ps_t0, ps_t1]
            smm0 = s["smm"].n
            se0 = s["sE"].n
            tr0 = s["sTr"].n
            pk0 = s["sPk"].n
            w0 = wst.i
            ov0 = ovst.i

            ko_t = s["sO"].n + NK * NCH * 16
            wb0 = wstb.i

            def _vcopy(sc, vi):
                sc.wait_ge(s["sTr"].h, tr0 + vi + 1)
                if vi >= 2:
                    ovst.wait(sc, ov0 + vi - 2)
                s["sPk"].inc(sc.copy(vstg[:, vi % 2, :, :], ps_tr[vi % 2][:, :]))
            forder = (list(range(NQ, NQ + NK)) + list(range(NQ)) +
                      list(range(NQ + NK, F_OUT)))

            @block.gpsimd
            def _(g):
                if pre_gate is not None:
                    g.wait_ge(s["skn"].h, pre_gate)
                for fi, f in enumerate(forder):
                    if fi >= 2:
                        g.wait_ge(s["smm"].h, smm0 + NCH * (fi - 1))
                    wst.issue(g, lambda f=f, fi=fi: g.dma_start(
                        out=wbuf[:, fi % 2, :, :], in_=wtile(f)))
                if ag_early is not None:
                    k_pair, v_pair, groups = ag_early
                    g.wait_ge(s["sO"].h, ko_t)
                    s["sCC"].inc(g.collective_compute(
                        "AllGather", ALU.bypass, replica_groups=groups,
                        ins=[k_pair[0]], outs=[k_pair[1]]), 1)

            @block.tensor
            def _(t):
                def _transpose(vi):
                    t.wait_ge(s["sE"].h, se0 + NCH * (NQ + NK) + vi + 1)
                    if vi >= 2:
                        t.wait_ge(s["sPk"].h, pk0 + vi - 1)
                    for tt in range(4):
                        i2 = t.transpose(ps_tr[vi % 2][:, tt * 128:(tt + 1) * 128],
                                         vscr[:, vi % 2, tt * 128:(tt + 1) * 128],
                                         ident[:, :])
                    s["sTr"].inc(i2)

                for fi, f in enumerate(forder):
                    wst.wait(t, w0 + fi)
                    for c in range(NCH):
                        g = NCH * fi + c
                        if g >= 4:
                            t.wait_ge(s["sE"].h, se0 + (g - 4) + 1)
                        for kt in range(DMT):
                            i2 = t.matmul(ps_main[g % 4][:, :], wbuf[:, fi % 2, kt, :],
                                          xn[:, kt, c * TC:(c + 1) * TC],
                                          start=(kt == 0), stop=(kt == DMT - 1))
                        s["smm"].inc(i2)
                        if f >= NQ + NK:
                            vi = (f - NQ - NK) * NCH + c
                            if vi >= 1:
                                _transpose(vi - 1)
                _transpose(NV * NCH - 1)

            @block.scalar
            def _(sc):
                for fi, f in enumerate(forder):
                    for c in range(NCH):
                        g = NCH * fi + c
                        sc.wait_ge(s["smm"].h, smm0 + g + 1)
                        if f < NQ:
                            i2 = sc.activation(q_sb[:, f, c * TC:(c + 1) * TC],
                                               ps_main[g % 4][:, :], AF.Identity,
                                               bias=bias_sb[:, f:f + 1], scale=QSCALE)
                        elif f < NQ + NK:
                            i2 = sc.activation(k_sb[:, f - NQ, c * TC:(c + 1) * TC],
                                               ps_main[g % 4][:, :], AF.Identity,
                                               bias=bias_sb[:, f:f + 1])
                        else:
                            vi = (f - NQ - NK) * NCH + c
                            if vi >= 2:
                                sc.wait_ge(s["sTr"].h, tr0 + vi - 1)
                            i2 = sc.activation(vscr[:, vi % 2, :], ps_main[g % 4][:, :],
                                               AF.Identity, bias=bias_sb[:, f:f + 1])
                            s["sE"].inc(i2)
                            if vi >= 1:
                                _vcopy(sc, vi - 1)
                            continue
                        s["sE"].inc(i2)
                _vcopy(sc, NV * NCH - 1)

            @block.sync
            def _(sy):
                for fi in range(NK):
                    kt = forder[fi] - NQ
                    for c in range(NCH):
                        sy.wait_ge(s["sE"].h, se0 + NCH * fi + c + 1)
                        s["sO"].inc(sy.dma_start(
                            out=_fm(o_k)[:, kt, c * TC:(c + 1) * TC],
                            in_=k_sb[:, kt, c * TC:(c + 1) * TC]), 16)
                assert s["sO"].n == ko_t
                for vi in range(NV * NCH):
                    vt, c = divmod(vi, NCH)
                    sy.wait_ge(s["sPk"].h, pk0 + vi + 1)
                    ovst.issue(sy, lambda vt=vt, c=c, vi=vi: sy.dma_start(
                        out=o_v.rearrange("(tt p) f -> p tt f", p=128)[
                            :, 4 * c:4 * c + 4, vt * 128:(vt + 1) * 128],
                        in_=vstg[:, vi % 2, :, :]))

        with nc.Block() as block, \
                nc.psum_tensor(f"{tag}ps_n0", [1, TC], F32) as ps_n0, \
                nc.psum_tensor(f"{tag}ps_n1", [1, TC], F32) as ps_n1:
            ps_n = [ps_n0, ps_n1]
            ssq0 = s["ssq"].n
            sn0 = s["sN"].n
            nv0 = s["sNv"].n
            qr0 = s["sQr"].n
            n0 = nst.i
            nper = NQ // NHEAD
            grplist = list(range(NHEAD, 2 * NHEAD)) + list(range(NHEAD))
            kn_last = n0 + NHEAD * NCH - 1

            @block.sync
            def _(sy):
                s["sO"].inc(sy.dma_start(out=_fm(o_q), in_=q_sb[:, :, :]), 16)
                idx = 0
                for grp in grplist:
                    role, h = divmod(grp, NHEAD)
                    for c in range(NCH):
                        sy.wait_ge(s["sQr"].h, qr0 + idx + 1)
                        dst = o_qn if role == 0 else o_kn
                        nst.issue(sy, lambda dst=dst, h=h, c=c, idx=idx: sy.dma_start(
                            out=dst[:, h * T + c * TC: h * T + (c + 1) * TC],
                            in_=nslot[:, idx % 4, :]))
                        idx += 1

            @block.scalar
            def _(sc):
                def _sqrts(gp):
                    for c in range(NCH):
                        idx = gp * NCH + c
                        sc.wait_ge(s["sNv"].h, nv0 + idx + 1)
                        s["sQr"].inc(sc.activation(nslot[:, idx % 4, :],
                                                   nslot[:, idx % 4, :], AF.Sqrt))
                for gp, grp in enumerate(grplist):
                    role, h = divmod(grp, NHEAD)
                    src = q_sb if role == 0 else k_sb
                    for j in range(nper):
                        ft = h * nper + j
                        e = gp * nper + j
                        if e >= 2:
                            sc.wait_ge(s["sN"].h, sn0 + (e - 1) * NCH)
                        s["ssq"].inc(sc.activation(nscr[:, e % 2, :], src[:, ft, :], AF.Square))
                    if gp >= 1:
                        _sqrts(gp - 1)
                _sqrts(2 * NHEAD - 1)

            @block.tensor
            def _(t):
                for gp, grp in enumerate(grplist):
                    if gp >= 1:
                        t.wait_ge(s["sNv"].h, nv0 + NCH * gp)
                    for j in range(nper):
                        e = gp * nper + j
                        t.wait_ge(s["ssq"].h, ssq0 + e + 1)
                        for c in range(NCH):
                            s["sN"].inc(t.matmul(ps_n[c][:, :], ones_bf[:, :],
                                                 nscr[:, e % 2, c * TC:(c + 1) * TC],
                                                 start=(j == 0), stop=(j == nper - 1)))

            @block.vector
            def _(v):
                idx = 0
                for gp, grp in enumerate(grplist):
                    v.wait_ge(s["sN"].h, sn0 + (gp + 1) * nper * NCH)
                    for c in range(NCH):
                        if idx >= 4:
                            nst.wait(v, n0 + idx - 4)
                        s["sNv"].inc(v.tensor_copy(nslot[:, idx % 4, :], ps_n[idx % NCH][:, :]))
                        idx += 1

        if ag_mid is not None:
            pairs, groups = ag_mid
            with nc.Block() as block:
                @block.gpsimd
                def _(g):
                    nst.wait(g, kn_last)
                    s["sCC"].inc(g.collective_compute(
                        "AllGather", ALU.bypass, replica_groups=groups,
                        ins=[pairs[0][0]], outs=[pairs[0][1]]), 1)
                    ovst.wait_all(g)
                    s["sCC"].inc(g.collective_compute(
                        "AllGather", ALU.bypass, replica_groups=groups,
                        ins=[pairs[1][0]], outs=[pairs[1][1]]), 1)

        with nc.Block() as block:
            @block.sync
            def _(sy):
                sy.wait_ge(s["sO"].h, s["sO"].n)
                ovst.wait_all(sy)
                nst.wait_all(sy)



def emit_attn(nc, st, s, sd, csb, io, layer, tag):
    if layer == 0:
        NHEAD, NDT, add_resid = 1, 8, True
    else:
        NHEAD, NDT, add_resid = 8, 2, False
    use_mask = (layer == 1)
    two_pass = use_mask
    NAV = NHEAD * NDT
    q_in = io["q_in"]
    v_in = io["v_in"]
    qn_in = io["qn_in"]
    kn_loads = io["kn_loads"]
    wotile = io["wotile"]
    bo_in = io["bo_in"]
    o_out = io["o_out"]
    mk_in = io.get("mk_in")
    rp_in = io.get("rp_in")
    ktile = io.get("ktile")
    k_loads = io.get("k_loads")
    ag_k = io.get("ag_k")
    ag_v = io.get("ag_v")
    ag_kn = io.get("ag_kn")
    wo_gate = io.get("wo_gate")
    ones_bf, ones_f, ones_bfr = csb["ones_bf"], csb["ones_f"], csb["ones_bfr"]

    with contextlib.ExitStack() as pst:
        E = pst.enter_context
        q_sb = E(nc.sbuf_tensor(f"{tag}q_sb", [128, NAV, T], BF16))
        e_sb = E(nc.sbuf_tensor(f"{tag}e_sb", [128, KTOT, TC], BF16))
        av_sb = E(nc.sbuf_tensor(f"{tag}av_sb", [128, NAV, TC], BF16))
        kbuf = E(nc.sbuf_tensor(f"{tag}kbuf", [128, 3, NDT, 128], BF16))
        vbuf = E(nc.sbuf_tensor(f"{tag}vbuf", [128, 3, 512], BF16))
        wobuf = E(nc.sbuf_tensor(f"{tag}wobuf", [128, 2, NAV, 128], BF16))
        rb_sb = E(nc.sbuf_tensor(f"{tag}rb_sb", [128, TC], F32))
        og_sb = E(nc.sbuf_tensor(f"{tag}og_sb", [128, 2, TC], F32))
        out_sb = E(nc.sbuf_tensor(f"{tag}out_sb", [128, 2, TC], F32))
        rp_sb = E(nc.sbuf_tensor(f"{tag}rp_sb", [128, 2, TC], F32))
        msk_sb = E(nc.sbuf_tensor(f"{tag}msk", [128, KTOT, TC], BF16)) if use_mask else None
        qnb = E(nc.sbuf_tensor(f"{tag}qnb", [1, 2, T], F32))
        kfull = E(nc.sbuf_tensor(f"{tag}kfull", [128, NDT, S_KEYS], BF16)) if two_pass else None
        ln2_sb = E(nc.sbuf_tensor(f"{tag}ln2", [1, TC], F32))
        m1row_sb = E(nc.sbuf_tensor(f"{tag}m1row", [1, TC], BF16))
        knb = E(nc.sbuf_tensor(f"{tag}knb", [1, S_KEYS], F32))
        kmax_sb = E(nc.sbuf_tensor(f"{tag}kmax", [1, NHEAD], F32))
        mrow_sb = E(nc.sbuf_tensor(f"{tag}mrow", [1, NHEAD, T], BF16))
        row_sb = E(nc.sbuf_tensor(f"{tag}row", [1, TC], F32))
        bias_sb = E(nc.sbuf_tensor(f"{tag}bias", [128, DMT], F32))
        ost, rst, kst, vst, wost, qnst = (sd["A2"], sd["C2"], sd["F3"], sd["G3"],
                                          sd["D2"], sd["B2"])
        sq0 = s["sq"].n
        kn0c = s["skn"].n
        km0 = s["sKm"].n
        mr20 = s["sMr2"].n
        qn0 = qnst.i
        o00 = ost.i
        r00 = rst.i

        with nc.Block() as block:
            @block.sync
            def _(sy):
                s["sq"].inc(sy.dma_start(out=q_sb[:, :, :], in_=_fm(q_in)), 16)
                for f in range(DMT):
                    s["sq"].inc(sy.dma_start(out=bias_sb[:, f:f + 1],
                                             in_=bo_in[f * 128:(f + 1) * 128, 0:1]), 16)
                sy.wait_ge(s["sq"].h, s["sq"].n)

            @block.vector
            def _(v):
                for h in range(NHEAD):
                    v.wait_ge(s["skn"].h, kn0c + 16 * len(kn_loads(0)) * (h + 1))
                    s["sKm"].inc(v.reduce_max(kmax_sb[:, h:h + 1], knb[:, :],
                                              axis=mybir.AxisListType.X))
                v.tensor_scalar(kmax_sb[:, :], kmax_sb[:, :], -1.0, None, ALU.mult)

            @block.gpsimd
            def _(g):
                if ag_kn is not None:
                    g.wait_ge(s["sCC"].h, ag_kn)
                for h in range(NHEAD):
                    if h >= 1:
                        g.wait_ge(s["sKm"].h, km0 + h)
                    for dst, src in kn_loads(h):
                        s["skn"].inc(g.dma_start(out=dst(knb), in_=src), 16)
                pk = io.get("post_kn")
                if pk is not None:
                    pk(g)

        with nc.Block() as block:
            @block.sync
            def _(sy):
                for h in range(NHEAD):
                    if h >= 2:
                        sy.wait_ge(s["sMr2"].h, mr20 + h - 1)
                    qnst.issue(sy, lambda h=h: sy.dma_start(
                        out=qnb[:, h % 2, :], in_=qn_in[:, h * T:(h + 1) * T]))

            @block.vector
            def _(v):
                for h in range(NHEAD):
                    qnst.wait(v, qn0 + h)
                    s["sMr2"].inc(v.tensor_scalar(mrow_sb[:, h, :], qnb[:, h % 2, :],
                                                  kmax_sb[:, h:h + 1], -8.0, ALU.mult, ALU.add))

        for qc in range(NCH):
            if use_mask:
                with nc.Block() as block:
                    @block.sync
                    def _(sy):
                        mk0 = s["sMk"].n
                        s["sMk"].inc(sy.dma_start(
                            out=msk_sb[:, :, :],
                            in_=mk_in[qc, :, :, :].rearrange("k p t -> p k t")), 16)
                        sy.wait_ge(s["sMk"].h, mk0 + 16)

            for h in range(NHEAD):
                sfx = f"{tag}{qc}_{h}"
                npass = 2 if two_pass else 1
                nav_banks = min(NDT, 4)
                with contextlib.ExitStack() as bst:
                    block = bst.enter_context(nc.Block())
                    ps_s0 = bst.enter_context(nc.psum_tensor(f"ps_s0{sfx}", [128, TC], F32))
                    ps_s1 = bst.enter_context(nc.psum_tensor(f"ps_s1{sfx}", [128, TC], F32))
                    ps_bc = bst.enter_context(nc.psum_tensor(f"ps_bc{sfx}", [128, TC], F32))
                    ps_sum = [bst.enter_context(nc.psum_tensor(f"ps_su{p}{sfx}", [1, TC], F32))
                              for p in range(npass)]
                    ps_av = [bst.enter_context(nc.psum_tensor(f"ps_a{j}{sfx}", [128, TC], F32))
                             for j in range(nav_banks)]
                    ps_sc = [ps_s0, ps_s1]
                    sc0 = s["sSc"].n
                    ev0 = s["sEv"].n
                    ex0 = s["sEx"].n
                    sm0 = s["sSm"].n
                    av0 = s["sAv"].n
                    ae0 = s["sAe"].n
                    k0 = kst.i
                    v0 = vst.i
                    n_half = (NDT + 3) // 4
                    nkl = len(k_loads(h)) if two_pass else None

                    @block.tensor
                    def _(t):
                        def emit_sum(t, p, kt):
                            t.wait_ge(s["sEx"].h, ex0 + p * KTOT + kt + 1)
                            s["sSm"].inc(t.matmul(ps_sum[p][:, :], ones_bf[:, :],
                                                  e_sb[:, kt, :],
                                                  start=(kt == 0), stop=(kt == KTOT - 1)))

                        for p in range(npass):
                            negm = (mrow_sb[:, h, qc * TC:(qc + 1) * TC] if p == 0
                                    else m1row_sb[:, :])
                            if p == 1:
                                t.wait_ge(s["sM1"].h, s["sM1"].n + 1)
                            for kt in range(KTOT):
                                si = p * KTOT + kt
                                if two_pass:
                                    if p == 0 and kt == 0:
                                        for jj in range(nkl):
                                            kst.wait(t, k0 + jj)
                                else:
                                    kst.wait(t, k0 + kt)
                                if si >= 2:
                                    t.wait_ge(s["sEx"].h, ex0 + si - 1)
                                for dt in range(NDT):
                                    lhs = (kfull[:, dt, kt * 128:(kt + 1) * 128] if two_pass
                                           else kbuf[:, (k0 + kt) % 3, dt, :])
                                    t.matmul(ps_sc[si % 2][:, :], lhs,
                                             q_sb[:, h * NDT + dt, qc * TC:(qc + 1) * TC],
                                             start=(dt == 0), stop=False)
                                i2 = t.matmul(ps_sc[si % 2][:, :], ones_bfr[:, :], negm,
                                              start=False, stop=True)
                                s["sSc"].inc(i2)
                                if kt >= 1:
                                    emit_sum(t, p, kt - 1)
                            emit_sum(t, p, KTOT - 1)
                        t.wait_ge(s["sRc"].h, s["sRc"].n + 1)
                        s["sRb"].inc(t.matmul(ps_bc[:, :], ones_f[:, :], row_sb[:, :],
                                              start=True, stop=True))
                        for half in range(n_half):
                            dts = range(half * 4, min(NDT, half * 4 + 4))
                            for kt in range(KTOT):
                                vst.wait(t, v0 + half * KTOT + kt)
                                for j, dt in enumerate(dts):
                                    if kt == 0 and half > 0:
                                        t.wait_ge(s["sAe"].h, ae0 + half * 4 + j - 4 + 1)
                                    i2 = t.matmul(ps_av[j][:, :],
                                                  vbuf[:, (v0 + half * KTOT + kt) % 3,
                                                       j * 128:(j + 1) * 128],
                                                  e_sb[:, kt, :],
                                                  start=(kt == 0), stop=(kt == KTOT - 1))
                                s["sAv"].inc(i2)

                    @block.sync
                    def _(sy):
                        hd0 = h * NDT * 128
                        if two_pass:
                            if ag_k is not None:
                                sy.wait_ge(s["sCC"].h, ag_k)
                            for dst, src in k_loads(h):
                                kst.issue(sy, lambda dst=dst, src=src: sy.dma_start(
                                    out=dst(kfull), in_=src))
                        else:
                            for kt in range(KTOT):
                                if kt >= 3:
                                    sy.wait_ge(s["sSc"].h, sc0 + kt - 2)
                                kst.issue(sy, lambda kt=kt: sy.dma_start(
                                    out=kbuf[:, (k0 + kt) % 3, :, :], in_=ktile(kt)))
                        if ag_v is not None:
                            sy.wait_ge(s["sCC"].h, ag_v)
                        for half in range(n_half):
                            w = min(NDT, half * 4 + 4) * 128 - half * 512
                            for kt in range(KTOT):
                                i = half * KTOT + kt
                                if i >= 3:
                                    sy.wait_ge(s["sAv"].h, av0 + i - 2)
                                vst.issue(sy, lambda kt=kt, half=half, w=w, i=i: sy.dma_start(
                                    out=vbuf[:, (v0 + i) % 3, 0:w],
                                    in_=v_in.rearrange("(kt p) f -> p kt f", p=128)[
                                        :, kt, hd0 + half * 512: hd0 + half * 512 + w]))

                    @block.vector
                    def _(v):
                        ch = s["chV"]
                        for p in range(npass):
                            if use_mask:
                                for kt in range(KTOT):
                                    si = p * KTOT + kt
                                    v.wait_ge(s["sSc"].h, sc0 + si + 1)
                                    s["sEv"].inc(v.tensor_tensor(
                                        ps_sc[si % 2][:, :], ps_sc[si % 2][:, :],
                                        msk_sb[:, kt, :], ALU.min))
                            if p == 0 and two_pass:
                                v.wait_ge(s["sSm"].h, sm0 + KTOT)
                                s["sCp1"].inc(v.tensor_scalar(row_sb[:, :], ps_sum[0][:, :],
                                                              float(2.0 ** 50), None, ALU.mult))
                                v.wait_ge(s["sLn"].h, s["sLn"].n + 1)
                                ch.inc(v.tensor_scalar(m1row_sb[:, :], ln2_sb[:, :], -8.0,
                                                       8.0 * 50.0 * math.log(2.0),
                                                       ALU.mult, ALU.add))
                                v.wait_ge(ch.h, ch.n)
                                s["sM1"].inc(v.tensor_add(m1row_sb[:, :], m1row_sb[:, :],
                                                          mrow_sb[:, h, qc * TC:(qc + 1) * TC]))
                        v.wait_ge(s["sSm"].h, s["sSm"].n)
                        ch.inc(v.tensor_copy(row_sb[:, :], ps_sum[npass - 1][:, :]))
                        v.wait_ge(ch.h, ch.n)
                        s["sRc"].inc(v.reciprocal(row_sb[:, :], row_sb[:, :]))
                        for half in range(n_half):
                            for j, dt in enumerate(range(half * 4, min(NDT, half * 4 + 4))):
                                v.wait_ge(s["sAv"].h, av0 + half * KTOT + KTOT)
                                if half == 0 and j == 0:
                                    v.wait_ge(s["sRbc"].h, s["sRbc"].n + 1)
                                s["sAe"].inc(v.tensor_mul(av_sb[:, h * NDT + dt, :],
                                                          ps_av[j][:, :], rb_sb[:, :]))

                    @block.scalar
                    def _(sc):
                        for p in range(npass):
                            escale = 0.125 if (two_pass and p == 0) else 1.0
                            for kt in range(KTOT):
                                si = p * KTOT + kt
                                if use_mask:
                                    sc.wait_ge(s["sEv"].h, ev0 + si + 1)
                                else:
                                    sc.wait_ge(s["sSc"].h, sc0 + si + 1)
                                if p == 1:
                                    sc.wait_ge(s["sSm"].h, sm0 + kt + 1)
                                s["sEx"].inc(sc.activation(e_sb[:, kt, :], ps_sc[si % 2][:, :],
                                                           AF.Exp, scale=escale))
                            if p == 0 and two_pass:
                                sc.wait_ge(s["sCp1"].h, s["sCp1"].n)
                                s["sLn"].inc(sc.activation(ln2_sb[:, :], row_sb[:, :], AF.Ln))
                        sc.wait_ge(s["sRb"].h, s["sRb"].n)
                        s["sRbc"].inc(sc.activation(rb_sb[:, :], ps_bc[:, :], AF.Identity))

            with nc.Block() as block, \
                    nc.psum_tensor(f"{tag}ps_o0{qc}", [128, TC], F32) as ps_o0, \
                    nc.psum_tensor(f"{tag}ps_o1{qc}", [128, TC], F32) as ps_o1:
                ps_o = [ps_o0, ps_o1]
                po0 = s["sPo"].n
                oe0 = s["sOe"].n
                oa0 = s["sOa"].n
                wo0 = wost.i

                @block.gpsimd
                def _(g):
                    if wo_gate is not None:
                        g.wait_ge(s["skn"].h, wo_gate)
                    for f in range(DMT):
                        if f >= 2:
                            g.wait_ge(s["sPo"].h, po0 + f - 1)
                        wost.issue(g, lambda f=f: g.dma_start(
                            out=wobuf[:, f % 2, :, :], in_=wotile(f)))

                @block.tensor
                def _(t):
                    for f in range(DMT):
                        wost.wait(t, wo0 + f)
                        if f >= 2:
                            t.wait_ge(s["sOe"].h, oe0 + f - 1)
                        for dt in range(NAV):
                            i2 = t.matmul(ps_o[f % 2][:, :], wobuf[:, f % 2, dt, :],
                                          av_sb[:, dt, :],
                                          start=(dt == 0), stop=(dt == NAV - 1))
                        s["sPo"].inc(i2)

                @block.scalar
                def _(sc):
                    for f in range(DMT):
                        sc.wait_ge(s["sPo"].h, po0 + f + 1)
                        gi = qc * DMT + f
                        if add_resid:
                            if f >= 2:
                                sc.wait_ge(s["sOa"].h, oa0 + f - 1)
                        elif gi >= 2:
                            ost.wait(sc, o00 + gi - 2)
                        s["sOe"].inc(sc.activation(og_sb[:, f % 2, :], ps_o[f % 2][:, :],
                                                   AF.Identity, bias=bias_sb[:, f:f + 1]))

                @block.vector
                def _(v):
                    if add_resid:
                        for f in range(DMT):
                            v.wait_ge(s["sOe"].h, oe0 + f + 1)
                            rst.wait(v, r00 + qc * DMT + f)
                            gi = qc * DMT + f
                            if gi >= 2:
                                ost.wait(v, o00 + gi - 2)
                            s["sOa"].inc(v.tensor_add(out_sb[:, f % 2, :], og_sb[:, f % 2, :],
                                                      rp_sb[:, f % 2, :]))

                @block.sync
                def _(sy):
                    src = out_sb if add_resid else og_sb
                    sem, base = (s["sOa"], oa0) if add_resid else (s["sOe"], oe0)
                    if add_resid:
                        rst.issue(sy, lambda: sy.dma_start(
                            out=rp_sb[:, 0, :], in_=_fm(rp_in)[:, 0, qc * TC:(qc + 1) * TC]))
                        rst.issue(sy, lambda: sy.dma_start(
                            out=rp_sb[:, 1, :], in_=_fm(rp_in)[:, 1, qc * TC:(qc + 1) * TC]))
                    for f in range(DMT):
                        sy.wait_ge(sem.h, base + f + 1)
                        ost.issue(sy, lambda f=f: sy.dma_start(
                            out=_fm(o_out)[:, f, qc * TC:(qc + 1) * TC],
                            in_=src[:, f % 2, :]))
                        if add_resid and f + 2 < DMT:
                            rst.issue(sy, lambda f=f: sy.dma_start(
                                out=rp_sb[:, f % 2, :],
                                in_=_fm(rp_in)[:, f + 2, qc * TC:(qc + 1) * TC]))

        with nc.Block() as block:
            @block.sync
            def _(sy):
                ost.wait_all(sy)



def emit_dequant(nc, st, s, sd, csb, io, tag):
    upc, upa, dnc, dna = io["upc"], io["upa"], io["dnc"], io["dna"]
    o_up, o_dn = io["o_up"], io["o_dn"]
    dqd = io["dqd"]
    ident = csb["ident"]
    cst_l, ast_l, out_st = sd["F3"], sd["G3"], sd["A2"]

    with contextlib.ExitStack() as pst:
        E = pst.enter_context
        cbuf = E(nc.sbuf_tensor(f"{tag}cbuf", [128, 2, 512], F32))
        abuf = E(nc.sbuf_tensor(f"{tag}abuf", [128, 2, 8], F32))
        mbuf = E(nc.sbuf_tensor(f"{tag}mbuf", [128, 4, 512], BF16))
        diag = E(nc.sbuf_tensor(f"{tag}diag", [128, 16, 256], BF16))
        vtile = E(nc.sbuf_tensor(f"{tag}vtile", [128, 2, 512], BF16))
        stg = E(nc.sbuf_tensor(f"{tag}stg", [128, 2, 4, 128], BF16))

        tiles = []
        for rt in range(RUP // 128):
            for jc in range(DM // 512):
                tiles.append(("up", rt, jc, 512, 8))
        for rt in range(DNR // 128):
            for jc in range(22):
                w = 512 if jc < 21 else 256
                tiles.append(("dn", rt, jc, w, w // 64))
        NTL = len(tiles)
        dq0 = s["sDq"].n
        tr0 = s["sTr"].n
        pk0 = s["sPk"].n
        pt0 = s["sPt"].n
        xp0 = s["sXp"].n
        dqe0 = s["sDe"].n
        c0 = cst_l.i
        a0 = ast_l.i
        o0 = out_st.i

        with nc.Block() as block:
            @block.sync
            def _(sy):
                sl0 = s["sL"].n
                s["sL"].inc(sy.dma_start(
                    out=diag[:, :, :],
                    in_=dqd.rearrange("(k p) n -> p k n", p=128)), 16)
                sy.wait_ge(s["sL"].h, s["sL"].n)

        with contextlib.ExitStack() as bst:
            block = bst.enter_context(nc.Block())
            ps_tr = [bst.enter_context(nc.psum_tensor(f"{tag}ps_t{j}", [128, 512], BF16))
                     for j in range(2)]
            ps_w = [bst.enter_context(nc.psum_tensor(f"{tag}ps_w{j}", [128, 512], F32))
                    for j in range(2)]

            def _store(g, i):
                kind, rt, jc, w, nb = tiles[i]
                g.wait_ge(s["sPk"].h, pk0 + i + 1)
                dst = o_up if kind == "up" else o_dn
                out_st.issue(g, lambda dst=dst, rt=rt, jc=jc, w=w, i=i: g.dma_start(
                    out=dst.rearrange("(t p) n -> p t n", p=128)[
                        :, jc * 4:jc * 4 + w // 128, rt * 128:(rt + 1) * 128],
                    in_=stg[:, i % 2, 0:w // 128, :]))

            @block.gpsimd
            def _(g):
                for i, (kind, rt, jc, w, nb) in enumerate(tiles):
                    if i >= 2:
                        g.wait_ge(s["sPt"].h, pt0 + 16 * (i - 1))
                    src = upc if kind == "up" else dnc
                    cst_l.issue(g, lambda src=src, rt=rt, jc=jc, w=w, i=i: g.dma_start(
                        out=cbuf[:, i % 2, 0:w],
                        in_=src.rearrange("(t p) n -> p t n", p=128)[:, rt, jc * 512:jc * 512 + w]))
                    if i >= 1:
                        _store(g, i - 1)
                _store(g, NTL - 1)

            @block.sync
            def _(sy):
                for i, (kind, rt, jc, w, nb) in enumerate(tiles):
                    if i >= 2:
                        sy.wait_ge(s["sDq"].h, dq0 + (i - 1))
                    src = upa if kind == "up" else dna
                    ast_l.issue(sy, lambda src=src, rt=rt, jc=jc, nb=nb, i=i: sy.dma_start(
                        out=abuf[:, i % 2, 0:nb],
                        in_=src.rearrange("(t p) n -> p t n", p=128)[:, rt, jc * 8:jc * 8 + nb]))

            @block.vector
            def _(v):
                for i, (kind, rt, jc, w, nb) in enumerate(tiles):
                    cst_l.wait(v, c0 + i)
                    ast_l.wait(v, a0 + i)
                    c_ap = cbuf[:, i % 2, 0:w]
                    for k in range(16):
                        mi = i * 16 + k
                        if mi >= 4:
                            v.wait_ge(s["sXp"].h, xp0 + mi - 3)
                        s["sPt"].inc(v.tensor_single_scalar(
                            mbuf[:, mi % 4, 0:w], c_ap, float(k), ALU.is_equal))
                    v.wait_ge(s["sXp"].h, xp0 + 16 * (i + 1))
                    a_b = bass.AP(abuf, abuf.shape[2] * (i % 2), [[2 * 8, 128], [1, nb], [0, 64]])
                    s["sDq"].inc(v.tensor_mul(vtile[:, i % 2, 0:w],
                                              ps_w[i % 2][:, 0:w], a_b))

            @block.tensor
            def _(t):
                for i, (kind, rt, jc, w, nb) in enumerate(tiles):
                    if i >= 2:
                        t.wait_ge(s["sDq"].h, dq0 + i - 1)
                    for k in range(16):
                        mi = i * 16 + k
                        t.wait_ge(s["sPt"].h, pt0 + mi + 1)
                        t.matmul(ps_w[i % 2][:, 0:w], diag[:, k, 0:128],
                                 mbuf[:, mi % 4, 0:w], start=(k == 0), stop=False)
                        i2 = t.matmul(ps_w[i % 2][:, 0:w], diag[:, k, 128:256],
                                      mbuf[:, mi % 4, 0:w], start=False, stop=(k == 15))
                        s["sXp"].inc(i2)
                    t.wait_ge(s["sDq"].h, dq0 + i + 1)
                    if i >= 2:
                        t.wait_ge(s["sPk"].h, pk0 + i - 1)
                    for tt in range(w // 128):
                        i2 = t.transpose(ps_tr[i % 2][:, tt * 128:(tt + 1) * 128],
                                         vtile[:, i % 2, tt * 128:(tt + 1) * 128], ident[:, :])
                    s["sTr"].inc(i2, 2)

            @block.scalar
            def _(sc):
                for i, (kind, rt, jc, w, nb) in enumerate(tiles):
                    sc.wait_ge(s["sTr"].h, tr0 + 2 * (i + 1))
                    if i >= 2:
                        out_st.wait(sc, o0 + i - 2)
                    s["sPk"].inc(sc.copy(stg[:, i % 2, 0:w // 128, :], ps_tr[i % 2][:, 0:w]))

        with nc.Block() as block:
            @block.sync
            def _(sy):
                out_st.wait_all(sy)



def emit_wtprep(nc, st, s, sd, csb, io, tag):
    ident = csb["ident"]
    lst, ost_ = sd["B2"], sd["C2"]

    with contextlib.ExitStack() as pst:
        E = pst.enter_context
        wld = E(nc.sbuf_tensor(f"{tag}wld", [128, 2, DMT, 128], BF16))
        wsg = E(nc.sbuf_tensor(f"{tag}wsg", [128, 2, 4, 128], BF16))

        for jn, (w_in, o_st2) in enumerate(io["jobs"]):
            R, C = w_in.shape
            nf = R // 128
            nk = C // 128
            npk = nk // 4
            with contextlib.ExitStack() as bst:
                block = bst.enter_context(nc.Block())
                ps_tr = [bst.enter_context(
                    nc.psum_tensor(f"{tag}ps{jn}_{j}", [128, 512], BF16)) for j in range(2)]
                tr0 = s["sTr"].n
                pk0 = s["sPk"].n
                l0 = lst.i
                o0 = ost_.i

                @block.gpsimd
                def _(g):
                    for f in range(nf):
                        if f >= 2:
                            g.wait_ge(s["sTr"].h, tr0 + npk * (f - 1))
                        lst.issue(g, lambda f=f: g.dma_start(
                            out=wld[:, f % 2, 0:nk, :],
                            in_=w_in.rearrange("(f p) (t n) -> p f t n", p=128, n=128)[
                                :, f, :, :]))

                @block.tensor
                def _(t):
                    for f in range(nf):
                        lst.wait(t, l0 + f)
                        for pk in range(npk):
                            i = f * npk + pk
                            if i >= 2:
                                t.wait_ge(s["sPk"].h, pk0 + i - 1)
                            for tt in range(4):
                                i2 = t.transpose(ps_tr[i % 2][:, tt * 128:(tt + 1) * 128],
                                                 wld[:, f % 2, pk * 4 + tt, :], ident[:, :])
                            s["sTr"].inc(i2)

                @block.scalar
                def _(sc):
                    for i in range(nf * npk):
                        sc.wait_ge(s["sTr"].h, tr0 + i + 1)
                        if o0 + i >= 2:
                            ost_.wait(sc, o0 + i - 2)
                        s["sPk"].inc(sc.copy(wsg[:, i % 2, :, :], ps_tr[i % 2][:, :]))

                @block.sync
                def _(sy):
                    for f in range(nf):
                        for pk in range(npk):
                            i = f * npk + pk
                            sy.wait_ge(s["sPk"].h, pk0 + i + 1)
                            ost_.issue(sy, lambda f=f, pk=pk, i=i: sy.dma_start(
                                out=o_st2.rearrange("(t p) n -> p t n", p=128)[
                                    :, pk * 4:pk * 4 + 4, f * 128:(f + 1) * 128],
                                in_=wsg[:, i % 2, :, :]))

        with nc.Block() as block:
            @block.sync
            def _(sy):
                ost_.wait_all(sy)



def emit_ffn0(nc, st, s, sd, csb, io, tag):
    xn_in, x_in, o_out = io["xn_in"], io["x_in"], io["o_out"]
    wuptile = io["wuptile"]
    wdntile = io["wdntile"]
    wdnpair = io["wdnpair"]
    bup, bdn = io["bup"], io["bdn"]
    ula, ulb, dla, dlb = io["ula"], io["ulb"], io["dla"], io["dlb"]
    wst, dst_s, rst, ost, xst = sd["A2"], sd["F3"], sd["C2"], sd["B2"], sd["H1"]
    wst2, dst2 = sd["E4"], sd["G3"]

    with contextlib.ExitStack() as pst:
        E = pst.enter_context
        xn = E(nc.sbuf_tensor(f"{tag}xn", [128, DMT, TC], BF16))
        x2 = E(nc.sbuf_tensor(f"{tag}x2", [128, HT, TC], BF16))
        wbuf = E(nc.sbuf_tensor(f"{tag}wbuf", [128, 2, DMT, 128], BF16))
        dbuf = E(nc.sbuf_tensor(f"{tag}dbuf", [128, 3, 2, 512], BF16))
        ulab = E(nc.sbuf_tensor(f"{tag}ulab", [128, DMT, 16], BF16))
        ulbb = E(nc.sbuf_tensor(f"{tag}ulbb", [16, HT * 128], BF16))
        dlab = E(nc.sbuf_tensor(f"{tag}dlab", [128, HT, 16], BF16))
        dlbb = E(nc.sbuf_tensor(f"{tag}dlbb", [16, DM], BF16))
        bupb = E(nc.sbuf_tensor(f"{tag}bupb", [128, HT], F32))
        bdnb = E(nc.sbuf_tensor(f"{tag}bdnb", [128, DMT], F32))
        u1 = E(nc.sbuf_tensor(f"{tag}u1", [16, TC], BF16))
        u2 = E(nc.sbuf_tensor(f"{tag}u2", [16, TC], BF16))
        og = E(nc.sbuf_tensor(f"{tag}og", [128, 2, TC], F32))
        out_sb = E(nc.sbuf_tensor(f"{tag}out_sb", [128, 2, TC], F32))
        rp_sb = E(nc.sbuf_tensor(f"{tag}rp_sb", [128, 2, TC], F32))

        l00 = s["sL"].n
        ll0 = s["sLl"].n
        u10 = s["sU1"].n
        u1e0 = s["sU1e"].n
        u20 = s["sU2"].n
        u2e0 = s["sU2e"].n
        x00 = xst.i
        o000 = ost.i

        with nc.Block() as block:
            @block.sync
            def _(sy):
                for f in range(HT):
                    s["sL"].inc(sy.dma_start(out=bupb[:, f:f + 1],
                                             in_=bup[f * 128:(f + 1) * 128, 0:1]), 16)
                for f in range(DMT):
                    s["sL"].inc(sy.dma_start(out=bdnb[:, f:f + 1],
                                             in_=bdn[f * 128:(f + 1) * 128, 0:1]), 16)
                sy.wait_ge(s["sL"].h, s["sL"].n)

            @block.gpsimd
            def _(g):
                s["sLl"].inc(g.dma_start(out=ulab[:, :, :],
                                         in_=ula.rearrange("(t p) r -> p t r", p=128)), 16)
                s["sLl"].inc(g.dma_start(out=dlab[:, :, :],
                                         in_=dla.rearrange("(t p) r -> p t r", p=128)), 16)
                s["sLl"].inc(g.dma_start(out=ulbb[:, :], in_=ulb), 16)
                s["sLl"].inc(g.dma_start(out=dlbb[:, :], in_=dlb), 16)
                g.wait_ge(s["sLl"].h, s["sLl"].n)

        for c in range(NCH):
            with nc.Block() as block, \
                    nc.psum_tensor(f"{tag}ps_u1{c}", [16, TC], F32) as ps_u1:
                @block.gpsimd
                def _(g):
                    xst.issue(g, lambda c=c: g.dma_start(
                        out=xn[:, :, :], in_=_fm(xn_in)[:, :, c * TC:(c + 1) * TC]))

                @block.tensor
                def _(t):
                    xst.wait(t, x00 + c)
                    for kt in range(DMT):
                        i2 = t.matmul(ps_u1[:, :], ulab[:, kt, :], xn[:, kt, :],
                                      start=(kt == 0), stop=(kt == DMT - 1))
                    s["sU1"].inc(i2)

                @block.scalar
                def _(sc):
                    sc.wait_ge(s["sU1"].h, s["sU1"].n)
                    s["sU1e"].inc(sc.activation(u1[:, :], ps_u1[:, :], AF.Identity))

            with contextlib.ExitStack() as bst:
                block = bst.enter_context(nc.Block())
                ps_up = [bst.enter_context(nc.psum_tensor(f"{tag}ps_up{c}_{j}", [128, TC], F32))
                         for j in range(4)]
                up0 = s["sUp"].n
                ue0 = s["sUe"].n
                w0 = wst.i
                w20 = wst2.i

                @block.gpsimd
                def _(g):
                    for f in range(0, HT, 2):
                        if f >= 2:
                            g.wait_ge(s["sUp"].h, up0 + f - 1)
                        wst.issue(g, lambda f=f: g.dma_start(
                            out=wbuf[:, 0, :, :], in_=wuptile(f)))

                @block.sync
                def _(sy):
                    for f in range(1, HT, 2):
                        if f >= 2:
                            sy.wait_ge(s["sUp"].h, up0 + f - 1)
                        wst2.issue(sy, lambda f=f: sy.dma_start(
                            out=wbuf[:, 1, :, :], in_=wuptile(f)))

                @block.tensor
                def _(t):
                    for f in range(HT):
                        if f % 2 == 0:
                            wst.wait(t, w0 + f // 2)
                        else:
                            wst2.wait(t, w20 + f // 2)
                        if f >= 4:
                            t.wait_ge(s["sUe"].h, ue0 + f - 3)
                        for kt in range(DMT):
                            t.matmul(ps_up[f % 4][:, :], wbuf[:, f % 2, kt, :],
                                     xn[:, kt, :], start=(kt == 0), stop=False)
                        i2 = t.matmul(ps_up[f % 4][:, :], ulbb[:, f * 128:(f + 1) * 128],
                                      u1[:, :], start=False, stop=True)
                        s["sUp"].inc(i2)

                @block.scalar
                def _(sc):
                    for f in range(HT):
                        sc.wait_ge(s["sUp"].h, up0 + f + 1)
                        s["sUe"].inc(sc.activation(x2[:, f, :], ps_up[f % 4][:, :],
                                                   AF.Relu, bias=bupb[:, f:f + 1]))

            with nc.Block() as block, \
                    nc.psum_tensor(f"{tag}ps_u2{c}", [16, TC], F32) as ps_u2:
                @block.tensor
                def _(t):
                    for kt in range(HT):
                        i2 = t.matmul(ps_u2[:, :], dlab[:, kt, :], x2[:, kt, :],
                                      start=(kt == 0), stop=(kt == HT - 1))
                    s["sU2"].inc(i2)

                @block.scalar
                def _(sc):
                    sc.wait_ge(s["sU2"].h, s["sU2"].n)
                    s["sU2e"].inc(sc.activation(u2[:, :], ps_u2[:, :], AF.Identity))

            with contextlib.ExitStack() as bst:
                block = bst.enter_context(nc.Block())
                ps_dn = [bst.enter_context(nc.psum_tensor(f"{tag}ps_dn{c}_{j}", [128, TC], F32))
                         for j in range(8)]
                dn0 = s["sDn"].n
                db0 = s["sDb"].n
                de0 = s["sDe"].n
                oa0 = s["sOa"].n
                d0 = dst_s.i
                d20 = dst2.i
                r0 = rst.i

                @block.sync
                def _(sy):
                    for i2 in range(4 * HT):
                        if i2 >= 3:
                            sy.wait_ge(s["sDb"].h, db0 + 2 * i2 - 4)
                        dst_s.issue(sy, lambda i2=i2: sy.dma_start(
                            out=dbuf[:, i2 % 3, :, :], in_=wdnpair(i2)))

                @block.gpsimd
                def _(g):
                    for f in range(DMT):
                        if f >= 2:
                            g.wait_ge(s["sOa"].h, oa0 + f - 1)
                        rst.issue(g, lambda f=f, c=c: g.dma_start(
                            out=rp_sb[:, f % 2, :],
                            in_=_fm(x_in)[:, f, c * TC:(c + 1) * TC]))
                        if f >= 1:
                            g.wait_ge(s["sOa"].h, oa0 + f)
                            ost.issue(g, lambda f=f, c=c: g.dma_start(
                                out=_fm(o_out)[:, f - 1, c * TC:(c + 1) * TC],
                                in_=out_sb[:, (f - 1) % 2, :]))
                    g.wait_ge(s["sOa"].h, oa0 + DMT)
                    ost.issue(g, lambda c=c: g.dma_start(
                        out=_fm(o_out)[:, DMT - 1, c * TC:(c + 1) * TC],
                        in_=out_sb[:, (DMT - 1) % 2, :]))

                @block.tensor
                def _(t):
                    for gq in range(8):
                        b4 = (gq % 2) * 4
                        if gq >= 2:
                            t.wait_ge(s["sDe"].h, de0 + (gq - 1) * 4)
                        for kt in range(HT):
                            i = gq * HT + kt
                            dst_s.wait(t, d0 + i // 2)
                            for d in range(4):
                                i2 = t.matmul(ps_dn[b4 + d][:, :],
                                              dbuf[:, (i // 2) % 3, i % 2,
                                                   d * 128:(d + 1) * 128],
                                              x2[:, kt, :], start=(kt == 0), stop=False)
                            s["sDb"].inc(i2)
                        t.wait_ge(s["sU2e"].h, u2e0 + c + 1)
                        for d in range(4):
                            gd = gq * 4 + d
                            s["sDn"].inc(t.matmul(
                                ps_dn[b4 + d][:, :], dlbb[:, gd * 128:(gd + 1) * 128],
                                u2[:, :], start=False, stop=True))

                @block.scalar
                def _(sc):
                    for gq in range(8):
                        for d in range(4):
                            f = gq * 4 + d
                            sc.wait_ge(s["sDn"].h, dn0 + gq * 4 + d + 1)
                            if f >= 2:
                                sc.wait_ge(s["sOa"].h, oa0 + f - 1)
                            s["sDe"].inc(sc.activation(og[:, f % 2, :],
                                                       ps_dn[(gq % 2) * 4 + d][:, :],
                                                       AF.Identity, bias=bdnb[:, f:f + 1]))

                @block.vector
                def _(v):
                    for f in range(DMT):
                        v.wait_ge(s["sDe"].h, de0 + f + 1)
                        rst.wait(v, r0 + f)
                        gi = c * DMT + f
                        if gi >= 2:
                            ost.wait(v, o000 + gi - 2)
                        s["sOa"].inc(v.tensor_add(out_sb[:, f % 2, :], og[:, f % 2, :],
                                                  rp_sb[:, f % 2, :]))

        with nc.Block() as block:
            @block.sync
            def _(sy):
                ost.wait_all(sy)



def emit_ffn1(nc, st, s, sd, csb, io, tag):
    xn_in, r1_in, a1_in, o_out = io["xn_in"], io["r1_in"], io["a1_in"], io["o_out"]
    up_in, dn_in = io["up_in"], io["dn_in"]
    permc = io["permc"]
    w_gate = io.get("w_gate")
    upst, dnst, r1st, a1st, ost = sd["F3"], sd["G3"], sd["C2"], sd["D2"], sd["A2"]
    r1v = r1_in.rearrange("r (c n) -> (r c) n", n=TC)
    a1v = a1_in.rearrange("r (c n) -> (r c) n", n=TC)
    ov = o_out.rearrange("r (c n) -> (r c) n", n=TC)
    NHH = NCH * 16

    with contextlib.ExitStack() as pst:
        E = pst.enter_context
        xp = E(nc.sbuf_tensor(f"{tag}xp", [128, DMT, T], BF16))
        yp = E(nc.sbuf_tensor(f"{tag}yp", [128, DMT, TC], F32))
        gbuf = E(nc.sbuf_tensor(f"{tag}gbuf", [128, 8, TC], BF16))
        upb = E(nc.sbuf_tensor(f"{tag}upb", [128, 2, 2, 1024], BF16))
        dnb = E(nc.sbuf_tensor(f"{tag}dnb", [128, 2, 8, 256], BF16))
        out_sb = E(nc.sbuf_tensor(f"{tag}out_sb", [128, 2, TC], F32))
        r1_sb = E(nc.sbuf_tensor(f"{tag}r1_sb", [128, 2, TC], F32))
        a1_sb = E(nc.sbuf_tensor(f"{tag}a1_sb", [128, 2, TC], F32))
        pidx = E(nc.sbuf_tensor(f"{tag}pidx", [128, DMT], I32))
        idxc = E(nc.sbuf_tensor(f"{tag}idxc", [128, 2, DMT], I32))

        pc0 = s["sPc"].n
        ptg0 = s["sPt"].n
        bc0 = s["sBc"].n
        o000 = ost.i

        with nc.Block() as block:
            @block.gpsimd
            def _(g):
                for gt in range(DMT):
                    s["sPc"].inc(g.dma_start(out=pidx[:, gt:gt + 1],
                                             in_=permc[gt * 128:(gt + 1) * 128, 0:1]), 16)
                g.wait_ge(s["sPc"].h, pc0 + 16 * DMT)
                for gt in range(DMT):
                    s["sPt"].inc(g.indirect_dma_start(
                        xp[:, gt, :], None,
                        xn_in, IndirectOffsetOnAxis(pidx[:, gt:gt + 1], 0)), 16)

            @block.vector
            def _(v):
                v.wait_ge(s["sPc"].h, pc0 + 16 * DMT)
                for c in range(NCH):
                    s["sBc"].inc(v.tensor_scalar(idxc[:, c, :], pidx[:, :],
                                                 2, c, ALU.mult, ALU.add))

        with contextlib.ExitStack() as bst:
            block = bst.enter_context(nc.Block())
            ps_up = [bst.enter_context(nc.psum_tensor(f"{tag}ps_hu{j}", [128, TC], F32))
                     for j in range(4)]
            ps_dn = [bst.enter_context(nc.psum_tensor(f"{tag}ps_hd{j}", [128, TC], F32))
                     for j in range(2)]
            up0 = s["sUp"].n
            ge0 = s["sGe"].n
            dn0 = s["sDn"].n
            ye0 = s["sYe"].n
            oa0 = s["sOa"].n
            u0 = upst.i
            d0 = dnst.i
            r0 = r1st.i
            a0 = a1st.i

            @block.sync
            def _(sy):
                if w_gate is not None:
                    sy.wait_ge(s["skn"].h, w_gate)
                for hh in range(NHH):
                    h = hh % 16
                    if hh >= 2:
                        sy.wait_ge(s["sUp"].h, up0 + (hh - 1) * 8)
                    upst.issue(sy, lambda h=h: sy.dma_start(
                        out=upb[:, h % 2, :, :],
                        in_=_fm(up_in)[:, 2 * h:2 * h + 2, :]))
                    if hh >= 2:
                        sy.wait_ge(s["sDn"].h, dn0 + (hh - 1) * 2)
                    dnst.issue(sy, lambda h=h: sy.dma_start(
                        out=dnb[:, h % 2, :, :],
                        in_=_fm(dn_in)[:, 8 * h:8 * h + 8, :]))

            @block.tensor
            def _(t):
                for hh in range(NHH):
                    c, h = divmod(hh, 16)
                    upst.wait(t, u0 + hh)
                    t.wait_ge(s["sPt"].h, ptg0 + 16 * (2 * h + 2))
                    for mt in range(8):
                        if hh * 8 + mt >= 4:
                            t.wait_ge(s["sGe"].h, ge0 + hh * 8 + mt - 3)
                        for kt in range(2):
                            i2 = t.matmul(ps_up[mt % 4][:, :],
                                          upb[:, h % 2, kt, mt * 128:(mt + 1) * 128],
                                          xp[:, 2 * h + kt, c * TC:(c + 1) * TC],
                                          start=(kt == 0), stop=(kt == 1))
                        s["sUp"].inc(i2)
                    dnst.wait(t, d0 + hh)
                    for dt in range(2):
                        if hh >= 1:
                            t.wait_ge(s["sYe"].h, ye0 + (hh - 1) * 2 + dt + 1)
                        for mt in range(8):
                            t.wait_ge(s["sGe"].h, ge0 + hh * 8 + mt + 1)
                            i2 = t.matmul(ps_dn[dt][:, :],
                                          dnb[:, h % 2, mt, dt * 128:(dt + 1) * 128],
                                          gbuf[:, mt, :],
                                          start=(mt == 0), stop=(mt == 7))
                        s["sDn"].inc(i2)

            @block.scalar
            def _(sc):
                for hh in range(NHH):
                    c, h = divmod(hh, 16)
                    for mt in range(8):
                        sc.wait_ge(s["sUp"].h, up0 + hh * 8 + mt + 1)
                        if hh >= 1:
                            sc.wait_ge(s["sDn"].h, dn0 + 2 * hh)
                        s["sGe"].inc(sc.activation(gbuf[:, mt, :], ps_up[mt % 4][:, :],
                                                   AF.Gelu))
                    for dt in range(2):
                        f = 2 * h + dt
                        sc.wait_ge(s["sDn"].h, dn0 + hh * 2 + dt + 1)
                        if c == 1:
                            sc.wait_ge(s["sOa"].h, oa0 + f + 1)
                        s["sYe"].inc(sc.activation(yp[:, f, :], ps_dn[dt][:, :],
                                                   AF.Identity))

            @block.vector
            def _(v):
                ch = s["chV"]
                for c in range(NCH):
                    for f in range(DMT):
                        gi = c * DMT + f
                        v.wait_ge(s["sYe"].h, ye0 + c * DMT + f + 1)
                        r1st.wait(v, r0 + gi)
                        a1st.wait(v, a0 + gi)
                        if gi >= 2:
                            ost.wait(v, o000 + gi - 2)
                        ch.inc(v.tensor_add(out_sb[:, f % 2, :], yp[:, f, :],
                                            r1_sb[:, f % 2, :]))
                        v.wait_ge(ch.h, ch.n)
                        s["sOa"].inc(v.tensor_add(out_sb[:, f % 2, :], out_sb[:, f % 2, :],
                                                  a1_sb[:, f % 2, :]))

            @block.gpsimd
            def _(g):
                g.wait_ge(s["sBc"].h, bc0 + NCH)
                for c in range(NCH):
                    for f in range(DMT):
                        gi = c * DMT + f
                        if gi >= 2:
                            g.wait_ge(s["sOa"].h, oa0 + gi - 1)
                        r1st.issue(g, lambda f=f, c=c: g.indirect_dma_start(
                            r1_sb[:, f % 2, :], None,
                            r1v, IndirectOffsetOnAxis(idxc[:, c, f:f + 1], 0)))
                        a1st.issue(g, lambda f=f, c=c: g.indirect_dma_start(
                            a1_sb[:, f % 2, :], None,
                            a1v, IndirectOffsetOnAxis(idxc[:, c, f:f + 1], 0)))
                        if gi >= 1:
                            cc, ff = divmod(gi - 1, DMT)
                            g.wait_ge(s["sOa"].h, oa0 + gi)
                            ost.issue(g, lambda cc=cc, ff=ff, gi=gi: g.indirect_dma_start(
                                ov, IndirectOffsetOnAxis(idxc[:, cc, ff:ff + 1], 0),
                                out_sb[:, (gi - 1) % 2, :], None))
                g.wait_ge(s["sOa"].h, oa0 + NCH * DMT)
                ost.issue(g, lambda: g.indirect_dma_start(
                    ov, IndirectOffsetOnAxis(idxc[:, NCH - 1, DMT - 1:DMT], 0),
                    out_sb[:, (NCH * DMT - 1) % 2, :], None))

        with nc.Block() as block:
            @block.sync
            def _(sy):
                ost.wait_all(sy)


def emit_ag(nc, s, pairs, groups, wait=True):
    with nc.Block() as block:
        @block.gpsimd
        def _(g):
            for i, o in pairs:
                s["sCC"].inc(g.collective_compute(
                    "AllGather", ALU.bypass, replica_groups=groups,
                    ins=[i], outs=[o]), 1)
            if wait:
                g.wait_ge(s["sCC"].h, s["sCC"].n)
    return s["sCC"].n


def emit_ag_wait(nc, s, upto):
    with nc.Block() as block:
        @block.gpsimd
        def _(g):
            g.wait_ge(s["sCC"].h, upto)



G8 = [list(range(8))]
G4 = [[0, 1, 2, 3], [4, 5, 6, 7]]


def build_fused():
    nc = bass.Bass("TRN2", target_bir_lowering=False)
    ap = lambda *a, **k: nc.dram_tensor(*a, **k).ap()
    x_in = ap("x_in", [DM, T], F32, kind="ExternalInput")
    wqkv0 = ap("wqkv0", [DM, 3072], F32, kind="ExternalInput")
    bq0 = ap("bq0", [3072, 1], F32, kind="ExternalInput")
    wo0 = ap("wo0", [1024, DM], F32, kind="ExternalInput")
    bo0 = ap("bo0", [DM, 1], F32, kind="ExternalInput")
    upc = ap("upc", [RUP, DM], I32, kind="ExternalInput")
    upa = ap("upa", [RUP, 64], F32, kind="ExternalInput")
    dnc = ap("dnc", [DNR, 11008], I32, kind="ExternalInput")
    dna = ap("dna", [DNR, 172], F32, kind="ExternalInput")
    bup = ap("bup", [11008, 1], F32, kind="ExternalInput")
    bdn = ap("bdn", [DM, 1], F32, kind="ExternalInput")
    ula = ap("ula", [DM, 16], F32, kind="ExternalInput")
    ulb = ap("ulb", [16, 11008], F32, kind="ExternalInput")
    dla = ap("dla", [11008, 16], F32, kind="ExternalInput")
    dlb = ap("dlb", [16, DM], F32, kind="ExternalInput")
    wq1s = ap("wq1s", [768, DM], F32, kind="ExternalInput")
    bq1 = ap("bq1", [6144, 1], F32, kind="ExternalInput")
    wo1s = ap("wo1s", [512, 2048], F32, kind="ExternalInput")
    bo1 = ap("bo1", [DM, 1], F32, kind="ExternalInput")
    up1 = ap("up1", [DM, 1024], F32, kind="ExternalInput")
    dn1 = ap("dn1", [16384, 256], F32, kind="ExternalInput")
    permr = ap("permr", [1, DM], I32, kind="ExternalInput")
    permc = ap("permc", [DM, 1], I32, kind="ExternalInput")
    c_jrow = ap("c_jrow", [1, DM], F32, kind="ExternalInput")
    mk_in = ap("mk_in", [NCH, KTOT, 128, TC], BF16, kind="ExternalInput")
    c_dqd = ap("c_dqd", [16 * 128, 256], BF16, kind="ExternalInput")
    c_ident = ap("c_ident", [128, 128], BF16, kind="ExternalInput")
    c_ones_bf = ap("c_ones_bf", [128, 1], BF16, kind="ExternalInput")
    c_ones_bfr = ap("c_ones_bfr", [1, 128], BF16, kind="ExternalInput")
    c_ones_f = ap("c_ones_f", [1, 128], F32, kind="ExternalInput")
    o_out = ap("o_out", [DM, T], F32, kind="ExternalOutput")
    dq_up_st = ap("dq_up_st", [DM, RUP], BF16)
    dq_dn_st = ap("dq_dn_st", [11008, DNR], BF16)
    wup_ag = ap("wup_ag", [8 * DM, RUP], BF16, addr_space="Shared")
    wdn_ag = ap("wdn_ag", [8 * 11008, DNR], BF16, addr_space="Shared")
    wq_st = ap("wq_st", [DM, 768], BF16)
    wq_ag = ap("wq_ag", [8 * DM, 768], BF16, addr_space="Shared")
    wo_st = ap("wo_st", [2048, 512], BF16)
    wo_ag = ap("wo_ag", [8 * 2048, 512], BF16, addr_space="Shared")
    xn0 = ap("xn0", [DM, T], BF16)
    up1_bf = ap("up1_bf", [DM, 1024], BF16)
    dn1_bf = ap("dn1_bf", [16384, 256], BF16)
    q0 = ap("q0", [1024, T], BF16)
    k0_st = ap("k0_st", [1024, T], BF16)
    v0_st = ap("v0_st", [T, 1024], BF16)
    qn0 = ap("qn0", [1, T], F32)
    kn0_st = ap("kn0_st", [1, T], F32)
    k0_ag = ap("k0_ag", [4 * 1024, T], BF16)
    v0_ag = ap("v0_ag", [4 * T, 1024], BF16)
    kn0_ag = ap("kn0_ag", [4, T], F32)
    v0fm = ap("v0fm", [1024, T], BF16)
    residp = ap("residp", [DM, T], F32)
    resid1 = ap("resid1", [DM, T], F32)
    xn1 = ap("xn1", [DM, T], BF16)
    q1 = ap("q1", [2048, T], BF16)
    k1_st = ap("k1_st", [2048, T], BF16)
    v1_st = ap("v1_st", [T, 2048], BF16)
    qn1 = ap("qn1", [1, 8 * T], F32)
    kn1_st = ap("kn1_st", [1, 8 * T], F32)
    k1_ag = ap("k1_ag", [4 * 2048, T], BF16)
    v1_ag = ap("v1_ag", [4 * T, 2048], BF16)
    kn1_ag = ap("kn1_ag", [4, 8 * T], F32)
    v1fm = ap("v1fm", [2048, T], BF16)
    attn1 = ap("attn1", [DM, T], F32)

    with contextlib.ExitStack() as st:
        E = st.enter_context
        csb = dict(
            ident=E(nc.sbuf_tensor("g_ident", [128, 128], BF16)),
            ones_bf=E(nc.sbuf_tensor("g_ones_bf", [128, 1], BF16)),
            ones_bfr=E(nc.sbuf_tensor("g_ones_bfr", [1, 128], BF16)),
            ones_f=E(nc.sbuf_tensor("g_ones_f", [1, 128], F32)),
        )
        s = _sems(nc, st, SEM_NAMES)
        sd = _mkstreams(nc, st)

        with nc.Block() as block:
            @block.sync
            def _(sy):
                s["sI"].inc(sy.dma_start(out=csb["ident"][:, :], in_=c_ident), 16)
                s["sI"].inc(sy.dma_start(out=csb["ones_bf"][:, :], in_=c_ones_bf), 16)
                s["sI"].inc(sy.dma_start(out=csb["ones_bfr"][:, :], in_=c_ones_bfr), 16)
                s["sI"].inc(sy.dma_start(out=csb["ones_f"][:, :], in_=c_ones_f), 16)
                sy.wait_ge(s["sI"].h, s["sI"].n)

        csba = {k: v[:, :] for k, v in csb.items()}

        emit_dequant(nc, st, s, sd, csba,
                     dict(upc=upc, upa=upa, dnc=dnc, dna=dna, dqd=c_dqd,
                          o_up=dq_up_st, o_dn=dq_dn_st), "dq")
        emit_wtprep(nc, st, s, sd, csba,
                    dict(jobs=[(wq1s, wq_st), (wo1s, wo_st)]), "wt")
        agw = emit_ag(nc, s, [(dq_up_st, wup_ag), (dq_dn_st, wdn_ag),
                              (wq_st, wq_ag), (wo_st, wo_ag)], G8, wait=False)

        emit_pre(nc, st, s, sd, csba,
                 dict(xin=x_in, bvec=bq0,
                      wtile=lambda f: _fm(wqkv0)[:, :, f * 128:(f + 1) * 128],
                      o_xn=xn0, o_q=q0, o_k=k0_st, o_v=v0_st,
                      o_qn=qn0, o_kn=kn0_st, v_fm=v0fm), 0, "p0")
        ag0 = emit_ag(nc, s, [(k0_st, k0_ag), (v0_st, v0_ag), (kn0_st, kn0_ag)], G4,
                      wait=False)
        emit_ag_wait(nc, s, agw)

        wupv = wup_ag.rearrange("(g t p) n -> p (g t) n", g=8, p=128)
        wdnv = wdn_ag.rearrange("(g t p) n -> p (g t) n", g=8, p=128)
        emit_ffn0(nc, st, s, sd, csba,
                  dict(xn_in=xn0, x_in=x_in, o_out=residp,
                       wuptile=lambda ht: wupv[:, (ht // 11) * 32:(ht // 11) * 32 + 32,
                                               (ht % 11) * 128:(ht % 11) * 128 + 128],
                       wdntile=lambda gq, kt: wdnv[:, gq * HT + kt, :],
                       wdnpair=lambda i2: wdnv[:, 2 * i2:2 * i2 + 2, :],
                       bup=bup, bdn=bdn, ula=ula, ulb=ulb, dla=dla, dlb=dlb), "f0")

        emit_ag_wait(nc, s, ag0)
        k0v = k0_ag.rearrange("(g d p) t -> p (g d) t", g=4, p=128)
        emit_attn(nc, st, s, sd, csba,
                  dict(q_in=q0, v_in=v0_ag, qn_in=qn0,
                       ktile=lambda kt: k0v[:, (kt // 8) * 8:(kt // 8) * 8 + 8,
                                            (kt % 8) * 128:(kt % 8) * 128 + 128],
                       kn_loads=lambda h: [
                           ((lambda knb, g=g: knb[0:1, g * T:(g + 1) * T]),
                            kn0_ag[g:g + 1, :]) for g in range(4)],
                       wotile=lambda f: _fm(wo0)[:, :, f * 128:(f + 1) * 128],
                       bo_in=bo0, rp_in=residp, o_out=resid1), 0, "a0")

        wq1v = wq_ag.rearrange("(g t p) n -> p (g t) n", g=8, p=128)
        emit_pre(nc, st, s, sd, csba,
                 dict(xin=resid1, bvec=bq1,
                      wtile=lambda f: wq1v[:, (f // 6) * 32:(f // 6) * 32 + 32,
                                           (f % 6) * 128:(f % 6) * 128 + 128],
                      o_xn=xn1, o_q=q1, o_k=k1_st, o_v=v1_st,
                      o_qn=qn1, o_kn=kn1_st, v_fm=v1fm,
                      ag_early=((k1_st, k1_ag), (v1_st, v1_ag), G4),
                      ag_mid=([(kn1_st, kn1_ag), (v1_st, v1_ag)], G4)), 1, "p1")
        ag_k1 = s["sCC"].n - 2
        ag_kn1 = s["sCC"].n - 1
        ag_v1 = s["sCC"].n

        k1v = k1_ag.rearrange("(g d p) t -> p (g d) t", g=4, p=128)
        wo1v = wo_ag.rearrange("(g t p) n -> p (g t) n", g=8, p=128)
        emit_attn(nc, st, s, sd, csba,
                  dict(q_in=q1, v_in=v1_ag, qn_in=qn1,
                       k_loads=lambda h: [
                           ((lambda kf, g=g: kf[:, :, g * 1024:(g + 1) * 1024]),
                            k1v[:, g * 16 + 2 * h:g * 16 + 2 * h + 2, :]) for g in range(4)],
                       kn_loads=lambda h: [
                           ((lambda knb, g=g: knb[0:1, g * T:(g + 1) * T]),
                            kn1_ag[g:g + 1, h * T:(h + 1) * T]) for g in range(4)],
                       wotile=lambda f: wo1v[:, (f // 4) * 16:(f // 4) * 16 + 16,
                                             (f % 4) * 128:(f % 4) * 128 + 128],
                       bo_in=bo1, mk_in=mk_in, o_out=attn1,
                       ag_k=ag_k1, ag_v=ag_v1, ag_kn=ag_kn1,
                       post_kn=lambda g: (
                           s["skn"].inc(g.dma_start(out=up1_bf, in_=up1), 16),
                           s["skn"].inc(g.dma_start(out=dn1_bf, in_=dn1), 16))), 1, "a1")
        wprecast_t = s["skn"].n

        emit_ffn1(nc, st, s, sd, csba,
                  dict(xn_in=xn1, r1_in=resid1, a1_in=attn1, o_out=o_out,
                       up_in=up1_bf, dn_in=dn1_bf, permc=permc,
                       w_gate=wprecast_t), "f1")
    return nc



def _cn(a, dt=None):
    a = np.ascontiguousarray(a)
    return a if dt is None else a.astype(dt)


def _make_masks(q0g):
    m = np.empty((NCH, KTOT, 128, TC), dtype=np.float32)
    for qc in range(NCH):
        tglob = q0g + qc * TC + np.arange(TC)[None, :]
        for kt in range(KTOT):
            fglob = kt * 128 + np.arange(128)[:, None]
            m[qc, kt] = np.where(tglob >= fglob, 1e30, MASKVAL)
    return m.astype(BF)


def kernel(L0_attn_x, L0_attn_Wqkv, L0_attn_bqkv, L0_attn_Wo, L0_attn_bo,
           L0_ffn_w_up_codes, L0_ffn_w_up_absmax, L0_ffn_b_up,
           L0_ffn_w_up_lora_a, L0_ffn_w_up_lora_b,
           L0_ffn_w_down_codes, L0_ffn_w_down_absmax, L0_ffn_b_down,
           L0_ffn_w_down_lora_a, L0_ffn_w_down_lora_b,
           L1_attn_w_qkv, L1_attn_b_qkv, L1_attn_w_o, L1_attn_b_o,
           L1_ffn_up, L1_ffn_down, L1_ffn_perm):
    cst = _consts_np()
    x = np.asarray(L0_attn_x, np.float32)
    upc = np.asarray(L0_ffn_w_up_codes, np.int32).reshape(11008, DM)
    upa = np.asarray(L0_ffn_w_up_absmax, np.float32).reshape(11008, 64)
    dnc = np.asarray(L0_ffn_w_down_codes, np.int32).reshape(DM, 11008)
    dna = np.asarray(L0_ffn_w_down_absmax, np.float32).reshape(DM, 172)
    upc_p = np.zeros((NCORES * RUP, DM), np.int32)
    upc_p[:11008] = upc
    upa_p = np.zeros((NCORES * RUP, 64), np.float32)
    upa_p[:11008] = upa
    wqkv0 = _cn(L0_attn_Wqkv, np.float32)
    wq1 = np.asarray(L1_attn_w_qkv, np.float32)
    wo1 = np.asarray(L1_attn_w_o, np.float32)
    perm = np.asarray(L1_ffn_perm, np.int32)
    jrow = np.arange(DM, dtype=np.float32).reshape(1, -1)

    in_maps = []
    for c in range(NCORES):
        bi, sl = divmod(c, 4)
        in_maps.append(dict(
            x_in=_cn(x[bi, sl * T:(sl + 1) * T, :].T, np.float32),
            wqkv0=wqkv0,
            bq0=_cn(np.asarray(L0_attn_bqkv, np.float32).reshape(-1, 1)),
            wo0=_cn(L0_attn_Wo, np.float32),
            bo0=_cn(np.asarray(L0_attn_bo, np.float32).reshape(-1, 1)),
            upc=_cn(upc_p[c * RUP:(c + 1) * RUP]),
            upa=_cn(upa_p[c * RUP:(c + 1) * RUP]),
            dnc=_cn(dnc[c * DNR:(c + 1) * DNR]),
            dna=_cn(dna[c * DNR:(c + 1) * DNR]),
            bup=_cn(np.asarray(L0_ffn_b_up, np.float32).reshape(-1, 1)),
            bdn=_cn(np.asarray(L0_ffn_b_down, np.float32).reshape(-1, 1)),
            ula=_cn(L0_ffn_w_up_lora_a, np.float32),
            ulb=_cn(L0_ffn_w_up_lora_b, np.float32),
            dla=_cn(L0_ffn_w_down_lora_a, np.float32),
            dlb=_cn(L0_ffn_w_down_lora_b, np.float32),
            wq1s=_cn(wq1[c * 768:(c + 1) * 768]),
            bq1=_cn(np.asarray(L1_attn_b_qkv, np.float32).reshape(-1, 1)),
            wo1s=_cn(wo1[c * 512:(c + 1) * 512]),
            bo1=_cn(np.asarray(L1_attn_b_o, np.float32).reshape(-1, 1)),
            up1=_cn(np.asarray(L1_ffn_up, np.float32).reshape(DM, 1024)),
            dn1=_cn(np.asarray(L1_ffn_down, np.float32).reshape(16384, 256)),
            permr=_cn(perm.reshape(1, -1)),
            permc=_cn(perm.reshape(-1, 1)),
            c_jrow=jrow,
            mk_in=_make_masks(sl * T),
            **cst))

    nc = build_fused()
    res = run_bass_kernel_spmd(nc, in_maps, core_ids=list(range(NCORES))).results
    out = np.empty((2, 4096, DM), np.float32)
    for c in range(NCORES):
        bi, sl = divmod(c, 4)
        out[bi, sl * T:(sl + 1) * T, :] = np.asarray(res[c]["o_out"]).T
    return out


# version 27
# speedup vs baseline: 1.0784x; 1.0784x over previous
import contextlib
import math
import numpy as np
import ml_dtypes
import concourse.bass as bass
import concourse.mybir as mybir
from concourse.bass_utils import run_bass_kernel_spmd
from concourse.bass import IndirectOffsetOnAxis

F32 = mybir.dt.float32
BF16 = mybir.dt.bfloat16
I32 = mybir.dt.int32
AF = mybir.ActivationFunctionType
ALU = mybir.AluOpType
BF = ml_dtypes.bfloat16

NCORES = 8
T = 1024
TC = 512
NCH = T // TC
DM = 4096
DMT = DM // 128
S_KEYS = 4096
KTOT = S_KEYS // 128
RUP = 1408
HT = 86
DNR = 512

NF4 = np.array([-1.0, -0.6961928009986877, -0.5250730514526367, -0.39491748809814453,
                -0.28444138169288635, -0.18477343022823334, -0.09105003625154495, 0.0,
                0.07958029955625534, 0.16093020141124725, 0.24611230194568634,
                0.33791524171829224, 0.44070982933044434, 0.5626170039176941,
                0.7229568362236023, 1.0], dtype=np.float32)
NF4_D = [float(np.float64(NF4[k]) - np.float64(NF4[k - 1])) for k in range(1, 16)]

EPS = 1e-5
MASKVAL = -10000.0


def _consts_np():
    hi = NF4.astype(BF)
    lo = (NF4.astype(np.float64) - hi.astype(np.float64)).astype(BF)
    dqd = np.zeros((16, 128, 256), dtype=BF)
    for k in range(16):
        dqd[k, :, 0:128] = np.eye(128, dtype=np.float32) * np.float32(hi[k])
        dqd[k, :, 128:256] = np.eye(128, dtype=np.float32) * np.float32(lo[k])
    return dict(
        c_ident=np.eye(128, dtype=BF),
        c_ones_bf=np.ones((128, 1), dtype=BF),
        c_ones_bfr=np.ones((1, 128), dtype=BF),
        c_ones_f=np.ones((1, 128), dtype=np.float32),
        c_dqd=dqd.reshape(16 * 128, 256),
    )


class _Cn:
    def __init__(self, h):
        self.h = h
        self.n = 0

    def inc(self, instr, k=1):
        instr.then_inc(self.h, k)
        self.n += k
        return instr


def _sems(nc, stack, names):
    return {nm: _Cn(stack.enter_context(nc.semaphore(nm))) for nm in names}


class DmaStream:

    def __init__(self, nc, stack, name, nslots):
        self.sems = [_Cn(stack.enter_context(nc.semaphore(f"{name}{k}")))
                     for k in range(nslots)]
        self.K = nslots
        self.i = 0

    def issue(self, eng, instr_fn):
        k = self.i % self.K
        if self.i >= self.K:
            eng.wait_ge(self.sems[k].h, 16 * (self.i // self.K))
        self.sems[k].inc(instr_fn(), 16)
        self.i += 1
        return self.i - 1

    def wait(self, eng, i):
        eng.wait_ge(self.sems[i % self.K].h, 16 * (i // self.K + 1))

    def wait_all(self, eng):
        for c in self.sems:
            if c.n:
                eng.wait_ge(c.h, c.n)


def _fm(ap):
    return ap.rearrange("(t p) n -> p t n", p=128)


SEM_NAMES = [
    "chV", "sCC",
    "sx", "ssq", "smm", "sE", "sTr", "sPk", "sN", "sNv", "sO", "sBr", "sQr",
    "sq", "skn", "sKm", "sMr", "sSc", "sEv", "sEx", "sSm", "sMb", "sRc", "sRb",
    "sRbc", "sAv", "sAe", "sPo", "sOe", "sOa", "sMk", "sMr2", "sCp1", "sLn",
    "sM1", "sBc1", "sI", "sDq", "sL", "sLl", "sU1", "sU1e", "sUp", "sUe",
    "sU2", "sU2e", "sDn", "sDb", "sDe", "sPc", "sBc", "sBe", "sPt", "sXp",
    "sXe", "sGe", "sYe", "sPt2", "sYn",
]


def _mkstreams(nc, st):
    return dict(
        A2=DmaStream(nc, st, "stA", 2), B2=DmaStream(nc, st, "stB", 2),
        C2=DmaStream(nc, st, "stC", 2), D2=DmaStream(nc, st, "stD", 2),
        E4=DmaStream(nc, st, "stE", 4), F3=DmaStream(nc, st, "stF", 3),
        G3=DmaStream(nc, st, "stG", 3), H1=DmaStream(nc, st, "stH", 1),
    )



def emit_pre(nc, st, s, sd, csb, io, layer, tag):
    if layer == 0:
        F_OUT, NQ, NK, NV, QSCALE, NHEAD = 24, 8, 8, 8, 1.0 / 32.0, 1
    else:
        F_OUT, NQ, NK, NV, QSCALE, NHEAD = 48, 16, 16, 16, 1.0, 8
    xin = io["xin"]
    wtile = io["wtile"]
    bvec = io["bvec"]
    o_xn, o_q, o_k, o_v = io["o_xn"], io["o_q"], io["o_k"], io["o_v"]
    o_qn, o_kn = io["o_qn"], io["o_kn"]
    v_fm = io["v_fm"]
    ag_mid = io.get("ag_mid")
    ag_early = io.get("ag_early")
    xin_nocast = io.get("xin_nocast", False)
    pre_gate = io.get("pre_gate")
    ones_bf, ones_f, ident = csb["ones_bf"], csb["ones_f"], csb["ident"]
    wst, vost, vlst, ovst, nst = sd["A2"], sd["B2"], sd["C2"], sd["D2"], sd["E4"]
    wstb = vlst

    with contextlib.ExitStack() as pst:
        E = pst.enter_context
        xn = E(nc.sbuf_tensor(f"{tag}xn", [128, DMT, T], BF16))
        q_sb = E(nc.sbuf_tensor(f"{tag}q_sb", [128, NQ, T], BF16))
        k_sb = E(nc.sbuf_tensor(f"{tag}k_sb", [128, NK, T], BF16))
        bias_sb = E(nc.sbuf_tensor(f"{tag}bias", [128, F_OUT], F32))
        sqb = E(nc.sbuf_tensor(f"{tag}sqb", [128, 2, TC], BF16))
        scr = E(nc.sbuf_tensor(f"{tag}scr", [128, TC], F32))
        srow = E(nc.sbuf_tensor(f"{tag}srow", [1, NCH, TC], F32))
        vrow = E(nc.sbuf_tensor(f"{tag}vrow", [1, NCH, TC], F32))
        m2row = E(nc.sbuf_tensor(f"{tag}m2row", [1, TC], F32))
        mb_sb = E(nc.sbuf_tensor(f"{tag}mb", [128, NCH, TC], F32))
        rb_sb = E(nc.sbuf_tensor(f"{tag}rb", [128, NCH, TC], F32))
        wbuf = E(nc.sbuf_tensor(f"{tag}wbuf", [128, 2, DMT, 128], BF16))
        vscr = E(nc.sbuf_tensor(f"{tag}vscr", [128, 2, TC], BF16))
        vld = E(nc.sbuf_tensor(f"{tag}vld", [128, 2, TC], BF16))
        vstg = E(nc.sbuf_tensor(f"{tag}vstg", [128, 2, 4, 128], BF16))
        nscr = E(nc.sbuf_tensor(f"{tag}nscr", [128, 2, T], BF16))
        nslot = E(nc.sbuf_tensor(f"{tag}nslot", [1, 4, TC], F32))

        sx0 = s["sx"].n
        so_bias0 = s["sO"].n

        with nc.Block() as block:
            @block.sync
            def _(sy):
                for f in range(F_OUT):
                    s["sO"].inc(sy.dma_start(out=bias_sb[:, f:f + 1],
                                             in_=bvec[f * 128:(f + 1) * 128, 0:1]), 16)
                sy.wait_ge(s["sO"].h, s["sO"].n)

            @block.gpsimd
            def _(g):
                if pre_gate is not None:
                    g.wait_ge(s["skn"].h, pre_gate)
                s["sx"].inc(g.dma_start(out=xn[:, :, :], in_=_fm(xin)), 16)
                g.wait_ge(s["sx"].h, sx0 + 16)

        if QSCALE != 1.0:
            with nc.Block() as block:
                @block.vector
                def _(v):
                    v.tensor_scalar(bias_sb[:, 0:NQ], bias_sb[:, 0:NQ], QSCALE, None, ALU.mult)

        with nc.Block() as block, \
                nc.psum_tensor(f"{tag}ps_s0", [1, TC], F32) as ps_s0, \
                nc.psum_tensor(f"{tag}ps_s1", [1, TC], F32) as ps_s1, \
                nc.psum_tensor(f"{tag}ps_q0", [1, TC], F32) as ps_q0, \
                nc.psum_tensor(f"{tag}ps_q1", [1, TC], F32) as ps_q1:
            ps_sum = [ps_s0, ps_s1]
            ps_ssq = [ps_q0, ps_q1]
            ssq0 = s["ssq"].n
            smm0 = s["smm"].n

            @block.scalar
            def _(sc):
                for c in range(NCH):
                    for kt in range(DMT):
                        i = c * DMT + kt
                        if i >= 2:
                            sc.wait_ge(s["smm"].h, smm0 + i - 1)
                        s["ssq"].inc(sc.activation(sqb[:, i % 2, :],
                                                   xn[:, kt, c * TC:(c + 1) * TC], AF.Square))

            @block.tensor
            def _(t):
                for c in range(NCH):
                    for kt in range(DMT):
                        i = c * DMT + kt
                        t.matmul(ps_sum[c][:, :], ones_bf[:, :], xn[:, kt, c * TC:(c + 1) * TC],
                                 start=(kt == 0), stop=(kt == DMT - 1))
                        t.wait_ge(s["ssq"].h, ssq0 + i + 1)
                        s["smm"].inc(t.matmul(ps_ssq[c][:, :], ones_bf[:, :], sqb[:, i % 2, :],
                                              start=(kt == 0), stop=(kt == DMT - 1)))

            @block.vector
            def _(v):
                ch = s["chV"]
                for c in range(NCH):
                    v.wait_ge(s["smm"].h, smm0 + DMT * (c + 1))
                    ch.inc(v.tensor_scalar(srow[:, c, :], ps_sum[c][:, :], 1.0 / DM, None, ALU.mult))
                    ch.inc(v.tensor_scalar(vrow[:, c, :], ps_ssq[c][:, :], 1.0 / DM, EPS, ALU.mult, ALU.add))
                    v.wait_ge(ch.h, ch.n)
                    ch.inc(v.tensor_mul(m2row[:, :], srow[:, c, :], srow[:, c, :]))
                    v.wait_ge(ch.h, ch.n)
                    ch.inc(v.tensor_sub(vrow[:, c, :], vrow[:, c, :], m2row[:, :]))

        with nc.Block() as block:
            @block.scalar
            def _(sc):
                sc.activation(vrow[:, :, :], vrow[:, :, :], AF.Sqrt)
        with nc.Block() as block:
            @block.vector
            def _(v):
                v.reciprocal(vrow[:, :, :], vrow[:, :, :])

        with nc.Block() as block, \
                nc.psum_tensor(f"{tag}ps_m0", [128, TC], F32) as ps_m0, \
                nc.psum_tensor(f"{tag}ps_m1", [128, TC], F32) as ps_m1, \
                nc.psum_tensor(f"{tag}ps_r0", [128, TC], F32) as ps_r0, \
                nc.psum_tensor(f"{tag}ps_r1", [128, TC], F32) as ps_r1:
            ps_m = [ps_m0, ps_m1]
            ps_r = [ps_r0, ps_r1]
            br0 = s["sBr"].n
            se0 = s["sE"].n
            sn0 = s["sN"].n

            @block.tensor
            def _(t):
                for c in range(NCH):
                    t.matmul(ps_m[c][:, :], ones_f[:, :], srow[:, c, :], start=True, stop=True)
                    s["sBr"].inc(t.matmul(ps_r[c][:, :], ones_f[:, :], vrow[:, c, :],
                                          start=True, stop=True))

            @block.scalar
            def _(sc):
                for c in range(NCH):
                    sc.wait_ge(s["sBr"].h, br0 + c + 1)
                    sc.activation(mb_sb[:, c, :], ps_m[c][:, :], AF.Identity)
                    s["sE"].inc(sc.activation(rb_sb[:, c, :], ps_r[c][:, :], AF.Identity))

            @block.vector
            def _(v):
                ch = s["chV"]
                for c in range(NCH):
                    v.wait_ge(s["sE"].h, se0 + c + 1)
                    for kt in range(DMT):
                        sl = xn[:, kt, c * TC:(c + 1) * TC]
                        if s["sN"].n > sn0:
                            v.wait_ge(s["sN"].h, s["sN"].n)
                        ch.inc(v.tensor_sub(scr[:, :], sl, mb_sb[:, c, :]))
                        v.wait_ge(ch.h, ch.n)
                        s["sN"].inc(v.tensor_mul(sl, scr[:, :], rb_sb[:, c, :]))

            @block.sync
            def _(sy):
                sy.wait_ge(s["sN"].h, sn0 + NCH * DMT)
                s["sO"].inc(sy.dma_start(out=_fm(o_xn), in_=xn[:, :, :]), 16)

        with nc.Block() as block, \
                nc.psum_tensor(f"{tag}ps_a", [128, TC], F32) as ps_a, \
                nc.psum_tensor(f"{tag}ps_b", [128, TC], F32) as ps_b, \
                nc.psum_tensor(f"{tag}ps_c", [128, TC], F32) as ps_c, \
                nc.psum_tensor(f"{tag}ps_d", [128, TC], F32) as ps_d, \
                nc.psum_tensor(f"{tag}ps_t0", [128, 512], BF16) as ps_t0, \
                nc.psum_tensor(f"{tag}ps_t1", [128, 512], BF16) as ps_t1:
            ps_main = [ps_a, ps_b, ps_c, ps_d]
            ps_tr = [ps_t0, ps_t1]
            smm0 = s["smm"].n
            se0 = s["sE"].n
            tr0 = s["sTr"].n
            pk0 = s["sPk"].n
            w0 = wst.i
            ov0 = ovst.i

            ko_t = s["sO"].n + NK * NCH * 16
            wb0 = wstb.i

            def _vcopy(sc, vi):
                sc.wait_ge(s["sTr"].h, tr0 + vi + 1)
                if vi >= 2:
                    ovst.wait(sc, ov0 + vi - 2)
                s["sPk"].inc(sc.copy(vstg[:, vi % 2, :, :], ps_tr[vi % 2][:, :]))
            forder = (list(range(NQ, NQ + NK)) + list(range(NQ)) +
                      list(range(NQ + NK, F_OUT)))

            @block.gpsimd
            def _(g):
                if pre_gate is not None:
                    g.wait_ge(s["skn"].h, pre_gate)
                for fi, f in enumerate(forder):
                    if fi >= 2:
                        g.wait_ge(s["smm"].h, smm0 + NCH * (fi - 1))
                    wst.issue(g, lambda f=f, fi=fi: g.dma_start(
                        out=wbuf[:, fi % 2, :, :], in_=wtile(f)))
                if ag_early is not None:
                    k_pair, v_pair, groups = ag_early
                    g.wait_ge(s["sO"].h, ko_t)
                    s["sCC"].inc(g.collective_compute(
                        "AllGather", ALU.bypass, replica_groups=groups,
                        ins=[k_pair[0]], outs=[k_pair[1]]), 1)

            @block.tensor
            def _(t):
                def _transpose(vi):
                    t.wait_ge(s["sE"].h, se0 + NCH * (NQ + NK) + vi + 1)
                    if vi >= 2:
                        t.wait_ge(s["sPk"].h, pk0 + vi - 1)
                    for tt in range(4):
                        i2 = t.transpose(ps_tr[vi % 2][:, tt * 128:(tt + 1) * 128],
                                         vscr[:, vi % 2, tt * 128:(tt + 1) * 128],
                                         ident[:, :])
                    s["sTr"].inc(i2)

                for fi, f in enumerate(forder):
                    wst.wait(t, w0 + fi)
                    for c in range(NCH):
                        g = NCH * fi + c
                        if g >= 4:
                            t.wait_ge(s["sE"].h, se0 + (g - 4) + 1)
                        for kt in range(DMT):
                            i2 = t.matmul(ps_main[g % 4][:, :], wbuf[:, fi % 2, kt, :],
                                          xn[:, kt, c * TC:(c + 1) * TC],
                                          start=(kt == 0), stop=(kt == DMT - 1))
                        s["smm"].inc(i2)
                        if f >= NQ + NK:
                            vi = (f - NQ - NK) * NCH + c
                            if vi >= 1:
                                _transpose(vi - 1)
                _transpose(NV * NCH - 1)

            @block.scalar
            def _(sc):
                for fi, f in enumerate(forder):
                    for c in range(NCH):
                        g = NCH * fi + c
                        sc.wait_ge(s["smm"].h, smm0 + g + 1)
                        if f < NQ:
                            i2 = sc.activation(q_sb[:, f, c * TC:(c + 1) * TC],
                                               ps_main[g % 4][:, :], AF.Identity,
                                               bias=bias_sb[:, f:f + 1], scale=QSCALE)
                        elif f < NQ + NK:
                            i2 = sc.activation(k_sb[:, f - NQ, c * TC:(c + 1) * TC],
                                               ps_main[g % 4][:, :], AF.Identity,
                                               bias=bias_sb[:, f:f + 1])
                        else:
                            vi = (f - NQ - NK) * NCH + c
                            if vi >= 2:
                                sc.wait_ge(s["sTr"].h, tr0 + vi - 1)
                            i2 = sc.activation(vscr[:, vi % 2, :], ps_main[g % 4][:, :],
                                               AF.Identity, bias=bias_sb[:, f:f + 1])
                            s["sE"].inc(i2)
                            if vi >= 1:
                                _vcopy(sc, vi - 1)
                            continue
                        s["sE"].inc(i2)
                _vcopy(sc, NV * NCH - 1)

            @block.sync
            def _(sy):
                for fi in range(NK):
                    kt = forder[fi] - NQ
                    for c in range(NCH):
                        sy.wait_ge(s["sE"].h, se0 + NCH * fi + c + 1)
                        s["sO"].inc(sy.dma_start(
                            out=_fm(o_k)[:, kt, c * TC:(c + 1) * TC],
                            in_=k_sb[:, kt, c * TC:(c + 1) * TC]), 16)
                assert s["sO"].n == ko_t
                for vi in range(NV * NCH):
                    vt, c = divmod(vi, NCH)
                    sy.wait_ge(s["sPk"].h, pk0 + vi + 1)
                    ovst.issue(sy, lambda vt=vt, c=c, vi=vi: sy.dma_start(
                        out=o_v.rearrange("(tt p) f -> p tt f", p=128)[
                            :, 4 * c:4 * c + 4, vt * 128:(vt + 1) * 128],
                        in_=vstg[:, vi % 2, :, :]))

        with nc.Block() as block, \
                nc.psum_tensor(f"{tag}ps_n0", [1, TC], F32) as ps_n0, \
                nc.psum_tensor(f"{tag}ps_n1", [1, TC], F32) as ps_n1:
            ps_n = [ps_n0, ps_n1]
            ssq0 = s["ssq"].n
            sn0 = s["sN"].n
            nv0 = s["sNv"].n
            qr0 = s["sQr"].n
            n0 = nst.i
            nper = NQ // NHEAD
            grplist = list(range(NHEAD, 2 * NHEAD)) + list(range(NHEAD))
            kn_last = n0 + NHEAD * NCH - 1

            @block.sync
            def _(sy):
                s["sO"].inc(sy.dma_start(out=_fm(o_q), in_=q_sb[:, :, :]), 16)
                idx = 0
                for grp in grplist:
                    role, h = divmod(grp, NHEAD)
                    for c in range(NCH):
                        sy.wait_ge(s["sQr"].h, qr0 + idx + 1)
                        dst = o_qn if role == 0 else o_kn
                        nst.issue(sy, lambda dst=dst, h=h, c=c, idx=idx: sy.dma_start(
                            out=dst[:, h * T + c * TC: h * T + (c + 1) * TC],
                            in_=nslot[:, idx % 4, :]))
                        idx += 1

            @block.scalar
            def _(sc):
                def _sqrts(gp):
                    for c in range(NCH):
                        idx = gp * NCH + c
                        sc.wait_ge(s["sNv"].h, nv0 + idx + 1)
                        s["sQr"].inc(sc.activation(nslot[:, idx % 4, :],
                                                   nslot[:, idx % 4, :], AF.Sqrt))
                for gp, grp in enumerate(grplist):
                    role, h = divmod(grp, NHEAD)
                    src = q_sb if role == 0 else k_sb
                    for j in range(nper):
                        ft = h * nper + j
                        e = gp * nper + j
                        if e >= 2:
                            sc.wait_ge(s["sN"].h, sn0 + (e - 1) * NCH)
                        s["ssq"].inc(sc.activation(nscr[:, e % 2, :], src[:, ft, :], AF.Square))
                    if gp >= 1:
                        _sqrts(gp - 1)
                _sqrts(2 * NHEAD - 1)

            @block.tensor
            def _(t):
                for gp, grp in enumerate(grplist):
                    if gp >= 1:
                        t.wait_ge(s["sNv"].h, nv0 + NCH * gp)
                    for j in range(nper):
                        e = gp * nper + j
                        t.wait_ge(s["ssq"].h, ssq0 + e + 1)
                        for c in range(NCH):
                            s["sN"].inc(t.matmul(ps_n[c][:, :], ones_bf[:, :],
                                                 nscr[:, e % 2, c * TC:(c + 1) * TC],
                                                 start=(j == 0), stop=(j == nper - 1)))

            @block.vector
            def _(v):
                idx = 0
                for gp, grp in enumerate(grplist):
                    v.wait_ge(s["sN"].h, sn0 + (gp + 1) * nper * NCH)
                    for c in range(NCH):
                        if idx >= 4:
                            nst.wait(v, n0 + idx - 4)
                        s["sNv"].inc(v.tensor_copy(nslot[:, idx % 4, :], ps_n[idx % NCH][:, :]))
                        idx += 1

        if ag_mid is not None:
            pairs, groups = ag_mid
            with nc.Block() as block:
                @block.gpsimd
                def _(g):
                    nst.wait(g, kn_last)
                    s["sCC"].inc(g.collective_compute(
                        "AllGather", ALU.bypass, replica_groups=groups,
                        ins=[pairs[0][0]], outs=[pairs[0][1]]), 1)
                    ovst.wait_all(g)
                    s["sCC"].inc(g.collective_compute(
                        "AllGather", ALU.bypass, replica_groups=groups,
                        ins=[pairs[1][0]], outs=[pairs[1][1]]), 1)

        with nc.Block() as block:
            @block.sync
            def _(sy):
                sy.wait_ge(s["sO"].h, s["sO"].n)
                ovst.wait_all(sy)
                nst.wait_all(sy)



def emit_attn(nc, st, s, sd, csb, io, layer, tag):
    if layer == 0:
        NHEAD, NDT, add_resid = 1, 8, True
    else:
        NHEAD, NDT, add_resid = 8, 2, False
    use_mask = (layer == 1)
    two_pass = use_mask
    NAV = NHEAD * NDT
    q_in = io["q_in"]
    v_in = io["v_in"]
    qn_in = io["qn_in"]
    kn_loads = io["kn_loads"]
    wotile = io["wotile"]
    bo_in = io["bo_in"]
    o_out = io["o_out"]
    mk_in = io.get("mk_in")
    rp_in = io.get("rp_in")
    ktile = io.get("ktile")
    k_loads = io.get("k_loads")
    ag_k = io.get("ag_k")
    ag_v = io.get("ag_v")
    ag_kn = io.get("ag_kn")
    wo_gate = io.get("wo_gate")
    ones_bf, ones_f, ones_bfr = csb["ones_bf"], csb["ones_f"], csb["ones_bfr"]

    with contextlib.ExitStack() as pst:
        E = pst.enter_context
        q_sb = E(nc.sbuf_tensor(f"{tag}q_sb", [128, NAV, T], BF16))
        e_sb = E(nc.sbuf_tensor(f"{tag}e_sb", [128, KTOT, TC], BF16))
        av_sb = E(nc.sbuf_tensor(f"{tag}av_sb", [128, NAV, TC], BF16))
        kbuf = E(nc.sbuf_tensor(f"{tag}kbuf", [128, 3, NDT, 128], BF16))
        vbuf = E(nc.sbuf_tensor(f"{tag}vbuf", [128, 3, 512], BF16))
        wobuf = E(nc.sbuf_tensor(f"{tag}wobuf", [128, 2, NAV, 128], BF16))
        rb_sb = E(nc.sbuf_tensor(f"{tag}rb_sb", [128, TC], F32))
        og_sb = E(nc.sbuf_tensor(f"{tag}og_sb", [128, 2, TC], F32))
        out_sb = E(nc.sbuf_tensor(f"{tag}out_sb", [128, 2, TC], F32))
        rp_sb = E(nc.sbuf_tensor(f"{tag}rp_sb", [128, 2, TC], F32))
        msk_sb = E(nc.sbuf_tensor(f"{tag}msk", [128, KTOT, TC], BF16)) if use_mask else None
        qnb = E(nc.sbuf_tensor(f"{tag}qnb", [1, 2, T], F32))
        kfull = E(nc.sbuf_tensor(f"{tag}kfull", [128, NDT, S_KEYS], BF16)) if two_pass else None
        ln2_sb = E(nc.sbuf_tensor(f"{tag}ln2", [1, TC], F32))
        m1row_sb = E(nc.sbuf_tensor(f"{tag}m1row", [1, TC], BF16))
        knb = E(nc.sbuf_tensor(f"{tag}knb", [1, S_KEYS], F32))
        kmax_sb = E(nc.sbuf_tensor(f"{tag}kmax", [1, NHEAD], F32))
        mrow_sb = E(nc.sbuf_tensor(f"{tag}mrow", [1, NHEAD, T], BF16))
        row_sb = E(nc.sbuf_tensor(f"{tag}row", [1, TC], F32))
        bias_sb = E(nc.sbuf_tensor(f"{tag}bias", [128, DMT], F32))
        ost, rst, kst, vst, wost, qnst = (sd["C2"], sd["D2"], sd["F3"], sd["G3"],
                                          sd["B2"], sd["E4"])
        sq0 = s["sq"].n
        kn0c = s["skn"].n
        km0 = s["sKm"].n
        mr20 = s["sMr2"].n
        qn0 = qnst.i
        o00 = ost.i
        r00 = rst.i

        with nc.Block() as block:
            @block.sync
            def _(sy):
                s["sq"].inc(sy.dma_start(out=q_sb[:, :, :], in_=_fm(q_in)), 16)
                for f in range(DMT):
                    s["sq"].inc(sy.dma_start(out=bias_sb[:, f:f + 1],
                                             in_=bo_in[f * 128:(f + 1) * 128, 0:1]), 16)
                sy.wait_ge(s["sq"].h, s["sq"].n)

            @block.vector
            def _(v):
                for h in range(NHEAD):
                    v.wait_ge(s["skn"].h, kn0c + 16 * len(kn_loads(0)) * (h + 1))
                    s["sKm"].inc(v.reduce_max(kmax_sb[:, h:h + 1], knb[:, :],
                                              axis=mybir.AxisListType.X))
                v.tensor_scalar(kmax_sb[:, :], kmax_sb[:, :], -1.0, None, ALU.mult)

            @block.gpsimd
            def _(g):
                if ag_kn is not None:
                    g.wait_ge(s["sCC"].h, ag_kn)
                for h in range(NHEAD):
                    if h >= 1:
                        g.wait_ge(s["sKm"].h, km0 + h)
                    for dst, src in kn_loads(h):
                        s["skn"].inc(g.dma_start(out=dst(knb), in_=src), 16)
                pk = io.get("post_kn")
                if pk is not None:
                    pk(g)

        with nc.Block() as block:
            @block.sync
            def _(sy):
                for h in range(NHEAD):
                    if h >= 2:
                        sy.wait_ge(s["sMr2"].h, mr20 + h - 1)
                    qnst.issue(sy, lambda h=h: sy.dma_start(
                        out=qnb[:, h % 2, :], in_=qn_in[:, h * T:(h + 1) * T]))

            @block.vector
            def _(v):
                for h in range(NHEAD):
                    qnst.wait(v, qn0 + h)
                    s["sMr2"].inc(v.tensor_scalar(mrow_sb[:, h, :], qnb[:, h % 2, :],
                                                  kmax_sb[:, h:h + 1], -8.0, ALU.mult, ALU.add))

        for qc in range(NCH):
            if use_mask:
                with nc.Block() as block:
                    @block.sync
                    def _(sy):
                        mk0 = s["sMk"].n
                        s["sMk"].inc(sy.dma_start(
                            out=msk_sb[:, :, :],
                            in_=mk_in[qc, :, :, :].rearrange("k p t -> p k t")), 16)
                        sy.wait_ge(s["sMk"].h, mk0 + 16)

            for h in range(NHEAD):
                sfx = f"{tag}{qc}_{h}"
                npass = 2 if two_pass else 1
                nav_banks = min(NDT, 4)
                with contextlib.ExitStack() as bst:
                    block = bst.enter_context(nc.Block())
                    ps_s0 = bst.enter_context(nc.psum_tensor(f"ps_s0{sfx}", [128, TC], F32))
                    ps_s1 = bst.enter_context(nc.psum_tensor(f"ps_s1{sfx}", [128, TC], F32))
                    ps_bc = bst.enter_context(nc.psum_tensor(f"ps_bc{sfx}", [128, TC], F32))
                    ps_sum = [bst.enter_context(nc.psum_tensor(f"ps_su{p}{sfx}", [1, TC], F32))
                              for p in range(npass)]
                    ps_av = [bst.enter_context(nc.psum_tensor(f"ps_a{j}{sfx}", [128, TC], F32))
                             for j in range(nav_banks)]
                    ps_sc = [ps_s0, ps_s1]
                    sc0 = s["sSc"].n
                    ev0 = s["sEv"].n
                    ex0 = s["sEx"].n
                    sm0 = s["sSm"].n
                    av0 = s["sAv"].n
                    ae0 = s["sAe"].n
                    k0 = kst.i
                    v0 = vst.i
                    n_half = (NDT + 3) // 4
                    nkl = len(k_loads(h)) if two_pass else None

                    @block.tensor
                    def _(t):
                        def emit_sum(t, p, kt):
                            t.wait_ge(s["sEx"].h, ex0 + p * KTOT + kt + 1)
                            s["sSm"].inc(t.matmul(ps_sum[p][:, :], ones_bf[:, :],
                                                  e_sb[:, kt, :],
                                                  start=(kt == 0), stop=(kt == KTOT - 1)))

                        for p in range(npass):
                            negm = (mrow_sb[:, h, qc * TC:(qc + 1) * TC] if p == 0
                                    else m1row_sb[:, :])
                            if p == 1:
                                t.wait_ge(s["sM1"].h, s["sM1"].n + 1)
                            for kt in range(KTOT):
                                si = p * KTOT + kt
                                if two_pass:
                                    if p == 0 and kt == 0:
                                        for jj in range(nkl):
                                            kst.wait(t, k0 + jj)
                                else:
                                    kst.wait(t, k0 + kt)
                                if si >= 2:
                                    t.wait_ge(s["sEx"].h, ex0 + si - 1)
                                for dt in range(NDT):
                                    lhs = (kfull[:, dt, kt * 128:(kt + 1) * 128] if two_pass
                                           else kbuf[:, (k0 + kt) % 3, dt, :])
                                    t.matmul(ps_sc[si % 2][:, :], lhs,
                                             q_sb[:, h * NDT + dt, qc * TC:(qc + 1) * TC],
                                             start=(dt == 0), stop=False)
                                i2 = t.matmul(ps_sc[si % 2][:, :], ones_bfr[:, :], negm,
                                              start=False, stop=True)
                                s["sSc"].inc(i2)
                                if kt >= 1:
                                    emit_sum(t, p, kt - 1)
                            emit_sum(t, p, KTOT - 1)
                        t.wait_ge(s["sRc"].h, s["sRc"].n + 1)
                        s["sRb"].inc(t.matmul(ps_bc[:, :], ones_f[:, :], row_sb[:, :],
                                              start=True, stop=True))
                        for half in range(n_half):
                            dts = range(half * 4, min(NDT, half * 4 + 4))
                            for kt in range(KTOT):
                                vst.wait(t, v0 + half * KTOT + kt)
                                for j, dt in enumerate(dts):
                                    if kt == 0 and half > 0:
                                        t.wait_ge(s["sAe"].h, ae0 + half * 4 + j - 4 + 1)
                                    i2 = t.matmul(ps_av[j][:, :],
                                                  vbuf[:, (v0 + half * KTOT + kt) % 3,
                                                       j * 128:(j + 1) * 128],
                                                  e_sb[:, kt, :],
                                                  start=(kt == 0), stop=(kt == KTOT - 1))
                                s["sAv"].inc(i2)

                    @block.sync
                    def _(sy):
                        hd0 = h * NDT * 128
                        if two_pass:
                            if ag_k is not None:
                                sy.wait_ge(s["sCC"].h, ag_k)
                            for dst, src in k_loads(h):
                                kst.issue(sy, lambda dst=dst, src=src: sy.dma_start(
                                    out=dst(kfull), in_=src))
                        else:
                            for kt in range(KTOT):
                                if kt >= 3:
                                    sy.wait_ge(s["sSc"].h, sc0 + kt - 2)
                                kst.issue(sy, lambda kt=kt: sy.dma_start(
                                    out=kbuf[:, (k0 + kt) % 3, :, :], in_=ktile(kt)))
                        if ag_v is not None:
                            sy.wait_ge(s["sCC"].h, ag_v)
                        for half in range(n_half):
                            w = min(NDT, half * 4 + 4) * 128 - half * 512
                            for kt in range(KTOT):
                                i = half * KTOT + kt
                                if i >= 3:
                                    sy.wait_ge(s["sAv"].h, av0 + i - 2)
                                vst.issue(sy, lambda kt=kt, half=half, w=w, i=i: sy.dma_start(
                                    out=vbuf[:, (v0 + i) % 3, 0:w],
                                    in_=v_in.rearrange("(kt p) f -> p kt f", p=128)[
                                        :, kt, hd0 + half * 512: hd0 + half * 512 + w]))

                    @block.vector
                    def _(v):
                        ch = s["chV"]
                        for p in range(npass):
                            if use_mask:
                                for kt in range(KTOT):
                                    si = p * KTOT + kt
                                    v.wait_ge(s["sSc"].h, sc0 + si + 1)
                                    s["sEv"].inc(v.tensor_tensor(
                                        ps_sc[si % 2][:, :], ps_sc[si % 2][:, :],
                                        msk_sb[:, kt, :], ALU.min))
                            if p == 0 and two_pass:
                                v.wait_ge(s["sSm"].h, sm0 + KTOT)
                                s["sCp1"].inc(v.tensor_scalar(row_sb[:, :], ps_sum[0][:, :],
                                                              float(2.0 ** 50), None, ALU.mult))
                                v.wait_ge(s["sLn"].h, s["sLn"].n + 1)
                                ch.inc(v.tensor_scalar(m1row_sb[:, :], ln2_sb[:, :], -8.0,
                                                       8.0 * 50.0 * math.log(2.0),
                                                       ALU.mult, ALU.add))
                                v.wait_ge(ch.h, ch.n)
                                s["sM1"].inc(v.tensor_add(m1row_sb[:, :], m1row_sb[:, :],
                                                          mrow_sb[:, h, qc * TC:(qc + 1) * TC]))
                        v.wait_ge(s["sSm"].h, s["sSm"].n)
                        ch.inc(v.tensor_copy(row_sb[:, :], ps_sum[npass - 1][:, :]))
                        v.wait_ge(ch.h, ch.n)
                        s["sRc"].inc(v.reciprocal(row_sb[:, :], row_sb[:, :]))
                        for half in range(n_half):
                            for j, dt in enumerate(range(half * 4, min(NDT, half * 4 + 4))):
                                v.wait_ge(s["sAv"].h, av0 + half * KTOT + KTOT)
                                if half == 0 and j == 0:
                                    v.wait_ge(s["sRbc"].h, s["sRbc"].n + 1)
                                s["sAe"].inc(v.tensor_mul(av_sb[:, h * NDT + dt, :],
                                                          ps_av[j][:, :], rb_sb[:, :]))

                    @block.scalar
                    def _(sc):
                        for p in range(npass):
                            escale = 0.125 if (two_pass and p == 0) else 1.0
                            for kt in range(KTOT):
                                si = p * KTOT + kt
                                if use_mask:
                                    sc.wait_ge(s["sEv"].h, ev0 + si + 1)
                                else:
                                    sc.wait_ge(s["sSc"].h, sc0 + si + 1)
                                if p == 1:
                                    sc.wait_ge(s["sSm"].h, sm0 + kt + 1)
                                s["sEx"].inc(sc.activation(e_sb[:, kt, :], ps_sc[si % 2][:, :],
                                                           AF.Exp, scale=escale))
                            if p == 0 and two_pass:
                                sc.wait_ge(s["sCp1"].h, s["sCp1"].n)
                                s["sLn"].inc(sc.activation(ln2_sb[:, :], row_sb[:, :], AF.Ln))
                        sc.wait_ge(s["sRb"].h, s["sRb"].n)
                        s["sRbc"].inc(sc.activation(rb_sb[:, :], ps_bc[:, :], AF.Identity))

            with nc.Block() as block, \
                    nc.psum_tensor(f"{tag}ps_o0{qc}", [128, TC], F32) as ps_o0, \
                    nc.psum_tensor(f"{tag}ps_o1{qc}", [128, TC], F32) as ps_o1:
                ps_o = [ps_o0, ps_o1]
                po0 = s["sPo"].n
                oe0 = s["sOe"].n
                oa0 = s["sOa"].n
                wo0 = wost.i

                @block.gpsimd
                def _(g):
                    if wo_gate is not None:
                        g.wait_ge(s["skn"].h, wo_gate)
                    src_sb = out_sb if add_resid else og_sb
                    sem, base = (s["sOa"], oa0) if add_resid else (s["sOe"], oe0)
                    for f in range(DMT):
                        if f >= 2:
                            g.wait_ge(s["sPo"].h, po0 + f - 1)
                        wost.issue(g, lambda f=f: g.dma_start(
                            out=wobuf[:, f % 2, :, :], in_=wotile(f)))
                        if f >= 2:
                            g.wait_ge(sem.h, base + f - 1)
                            ost.issue(g, lambda f=f: g.dma_start(
                                out=_fm(o_out)[:, f - 2, qc * TC:(qc + 1) * TC],
                                in_=src_sb[:, f % 2, :]))
                    for f in (DMT - 2, DMT - 1):
                        g.wait_ge(sem.h, base + f + 1)
                        ost.issue(g, lambda f=f: g.dma_start(
                            out=_fm(o_out)[:, f, qc * TC:(qc + 1) * TC],
                            in_=src_sb[:, f % 2, :]))

                @block.tensor
                def _(t):
                    for f in range(DMT):
                        wost.wait(t, wo0 + f)
                        if f >= 2:
                            t.wait_ge(s["sOe"].h, oe0 + f - 1)
                        for dt in range(NAV):
                            i2 = t.matmul(ps_o[f % 2][:, :], wobuf[:, f % 2, dt, :],
                                          av_sb[:, dt, :],
                                          start=(dt == 0), stop=(dt == NAV - 1))
                        s["sPo"].inc(i2)

                @block.scalar
                def _(sc):
                    for f in range(DMT):
                        sc.wait_ge(s["sPo"].h, po0 + f + 1)
                        gi = qc * DMT + f
                        if add_resid:
                            if f >= 2:
                                sc.wait_ge(s["sOa"].h, oa0 + f - 1)
                        elif gi >= 2:
                            ost.wait(sc, o00 + gi - 2)
                        s["sOe"].inc(sc.activation(og_sb[:, f % 2, :], ps_o[f % 2][:, :],
                                                   AF.Identity, bias=bias_sb[:, f:f + 1]))

                @block.vector
                def _(v):
                    if add_resid:
                        for f in range(DMT):
                            v.wait_ge(s["sOe"].h, oe0 + f + 1)
                            rst.wait(v, r00 + qc * DMT + f)
                            gi = qc * DMT + f
                            if gi >= 2:
                                ost.wait(v, o00 + gi - 2)
                            s["sOa"].inc(v.tensor_add(out_sb[:, f % 2, :], og_sb[:, f % 2, :],
                                                      rp_sb[:, f % 2, :]))

                @block.sync
                def _(sy):
                    if add_resid:
                        rst.issue(sy, lambda: sy.dma_start(
                            out=rp_sb[:, 0, :], in_=_fm(rp_in)[:, 0, qc * TC:(qc + 1) * TC]))
                        rst.issue(sy, lambda: sy.dma_start(
                            out=rp_sb[:, 1, :], in_=_fm(rp_in)[:, 1, qc * TC:(qc + 1) * TC]))
                        for f in range(DMT - 2):
                            sy.wait_ge(s["sOa"].h, oa0 + f + 1)
                            rst.issue(sy, lambda f=f: sy.dma_start(
                                out=rp_sb[:, f % 2, :],
                                in_=_fm(rp_in)[:, f + 2, qc * TC:(qc + 1) * TC]))

        with nc.Block() as block:
            @block.sync
            def _(sy):
                ost.wait_all(sy)



def emit_dequant(nc, st, s, sd, csb, io, tag):
    upc, upa, dnc, dna = io["upc"], io["upa"], io["dnc"], io["dna"]
    o_up, o_dn = io["o_up"], io["o_dn"]
    dqd = io["dqd"]
    ident = csb["ident"]
    cst_l, ast_l, out_st = sd["B2"], sd["G3"], sd["A2"]

    with contextlib.ExitStack() as pst:
        E = pst.enter_context
        cbuf = E(nc.sbuf_tensor(f"{tag}cbuf", [128, 2, 512], F32))
        abuf = E(nc.sbuf_tensor(f"{tag}abuf", [128, 2, 8], F32))
        mbuf = E(nc.sbuf_tensor(f"{tag}mbuf", [128, 4, 512], BF16))
        diag = E(nc.sbuf_tensor(f"{tag}diag", [128, 16, 256], BF16))
        vtile = E(nc.sbuf_tensor(f"{tag}vtile", [128, 2, 512], BF16))
        stg = E(nc.sbuf_tensor(f"{tag}stg", [128, 2, 4, 128], BF16))

        tiles = []
        for rt in range(RUP // 128):
            for jc in range(DM // 512):
                tiles.append(("up", rt, jc, 512, 8))
        for rt in range(DNR // 128):
            for jc in range(22):
                w = 512 if jc < 21 else 256
                tiles.append(("dn", rt, jc, w, w // 64))
        NTL = len(tiles)
        dq0 = s["sDq"].n
        tr0 = s["sTr"].n
        pk0 = s["sPk"].n
        pt0 = s["sPt"].n
        xp0 = s["sXp"].n
        dqe0 = s["sDe"].n
        c0 = cst_l.i
        a0 = ast_l.i
        o0 = out_st.i

        with nc.Block() as block:
            @block.sync
            def _(sy):
                sl0 = s["sL"].n
                s["sL"].inc(sy.dma_start(
                    out=diag[:, :, :],
                    in_=dqd.rearrange("(k p) n -> p k n", p=128)), 16)
                sy.wait_ge(s["sL"].h, s["sL"].n)

        with contextlib.ExitStack() as bst:
            block = bst.enter_context(nc.Block())
            ps_tr = [bst.enter_context(nc.psum_tensor(f"{tag}ps_t{j}", [128, 512], BF16))
                     for j in range(2)]
            ps_w = [bst.enter_context(nc.psum_tensor(f"{tag}ps_w{j}", [128, 512], F32))
                    for j in range(2)]

            def _store(g, i):
                kind, rt, jc, w, nb = tiles[i]
                g.wait_ge(s["sPk"].h, pk0 + i + 1)
                dst = o_up if kind == "up" else o_dn
                out_st.issue(g, lambda dst=dst, rt=rt, jc=jc, w=w, i=i: g.dma_start(
                    out=dst.rearrange("(t p) n -> p t n", p=128)[
                        :, jc * 4:jc * 4 + w // 128, rt * 128:(rt + 1) * 128],
                    in_=stg[:, i % 2, 0:w // 128, :]))

            @block.gpsimd
            def _(g):
                for i, (kind, rt, jc, w, nb) in enumerate(tiles):
                    if i >= 2:
                        g.wait_ge(s["sPt"].h, pt0 + 16 * (i - 1))
                    src = upc if kind == "up" else dnc
                    cst_l.issue(g, lambda src=src, rt=rt, jc=jc, w=w, i=i: g.dma_start(
                        out=cbuf[:, i % 2, 0:w],
                        in_=src.rearrange("(t p) n -> p t n", p=128)[:, rt, jc * 512:jc * 512 + w]))
                    if i >= 1:
                        _store(g, i - 1)
                _store(g, NTL - 1)

            @block.sync
            def _(sy):
                for i, (kind, rt, jc, w, nb) in enumerate(tiles):
                    if i >= 2:
                        sy.wait_ge(s["sDq"].h, dq0 + (i - 1))
                    src = upa if kind == "up" else dna
                    ast_l.issue(sy, lambda src=src, rt=rt, jc=jc, nb=nb, i=i: sy.dma_start(
                        out=abuf[:, i % 2, 0:nb],
                        in_=src.rearrange("(t p) n -> p t n", p=128)[:, rt, jc * 8:jc * 8 + nb]))

            @block.vector
            def _(v):
                for i, (kind, rt, jc, w, nb) in enumerate(tiles):
                    cst_l.wait(v, c0 + i)
                    ast_l.wait(v, a0 + i)
                    c_ap = cbuf[:, i % 2, 0:w]
                    for k in range(16):
                        mi = i * 16 + k
                        if mi >= 4:
                            v.wait_ge(s["sXp"].h, xp0 + mi - 3)
                        s["sPt"].inc(v.tensor_single_scalar(
                            mbuf[:, mi % 4, 0:w], c_ap, float(k), ALU.is_equal))
                    v.wait_ge(s["sXp"].h, xp0 + 16 * (i + 1))
                    a_b = bass.AP(abuf, abuf.shape[2] * (i % 2), [[2 * 8, 128], [1, nb], [0, 64]])
                    s["sDq"].inc(v.tensor_mul(vtile[:, i % 2, 0:w],
                                              ps_w[i % 2][:, 0:w], a_b))

            @block.tensor
            def _(t):
                for i, (kind, rt, jc, w, nb) in enumerate(tiles):
                    if i >= 2:
                        t.wait_ge(s["sDq"].h, dq0 + i - 1)
                    for k in range(16):
                        mi = i * 16 + k
                        t.wait_ge(s["sPt"].h, pt0 + mi + 1)
                        t.matmul(ps_w[i % 2][:, 0:w], diag[:, k, 0:128],
                                 mbuf[:, mi % 4, 0:w], start=(k == 0), stop=False)
                        i2 = t.matmul(ps_w[i % 2][:, 0:w], diag[:, k, 128:256],
                                      mbuf[:, mi % 4, 0:w], start=False, stop=(k == 15))
                        s["sXp"].inc(i2)
                    t.wait_ge(s["sDq"].h, dq0 + i + 1)
                    if i >= 2:
                        t.wait_ge(s["sPk"].h, pk0 + i - 1)
                    for tt in range(w // 128):
                        i2 = t.transpose(ps_tr[i % 2][:, tt * 128:(tt + 1) * 128],
                                         vtile[:, i % 2, tt * 128:(tt + 1) * 128], ident[:, :])
                    s["sTr"].inc(i2, 2)

            @block.scalar
            def _(sc):
                for i, (kind, rt, jc, w, nb) in enumerate(tiles):
                    sc.wait_ge(s["sTr"].h, tr0 + 2 * (i + 1))
                    if i >= 2:
                        out_st.wait(sc, o0 + i - 2)
                    s["sPk"].inc(sc.copy(stg[:, i % 2, 0:w // 128, :], ps_tr[i % 2][:, 0:w]))

        with nc.Block() as block:
            @block.sync
            def _(sy):
                out_st.wait_all(sy)



def emit_wtprep(nc, st, s, sd, csb, io, tag):
    ident = csb["ident"]
    lst, ost_ = sd["C2"], sd["F3"]

    with contextlib.ExitStack() as pst:
        E = pst.enter_context
        wld = E(nc.sbuf_tensor(f"{tag}wld", [128, 2, DMT, 128], BF16))
        wsg = E(nc.sbuf_tensor(f"{tag}wsg", [128, 2, 4, 128], BF16))

        for jn, (w_in, o_st2) in enumerate(io["jobs"]):
            R, C = w_in.shape
            nf = R // 128
            nk = C // 128
            npk = nk // 4
            with contextlib.ExitStack() as bst:
                block = bst.enter_context(nc.Block())
                ps_tr = [bst.enter_context(
                    nc.psum_tensor(f"{tag}ps{jn}_{j}", [128, 512], BF16)) for j in range(2)]
                tr0 = s["sTr"].n
                pk0 = s["sPk"].n
                l0 = lst.i
                o0 = ost_.i

                @block.gpsimd
                def _(g):
                    for f in range(nf):
                        if f >= 2:
                            g.wait_ge(s["sTr"].h, tr0 + npk * (f - 1))
                        lst.issue(g, lambda f=f: g.dma_start(
                            out=wld[:, f % 2, 0:nk, :],
                            in_=w_in.rearrange("(f p) (t n) -> p f t n", p=128, n=128)[
                                :, f, :, :]))

                @block.tensor
                def _(t):
                    for f in range(nf):
                        lst.wait(t, l0 + f)
                        for pk in range(npk):
                            i = f * npk + pk
                            if i >= 2:
                                t.wait_ge(s["sPk"].h, pk0 + i - 1)
                            for tt in range(4):
                                i2 = t.transpose(ps_tr[i % 2][:, tt * 128:(tt + 1) * 128],
                                                 wld[:, f % 2, pk * 4 + tt, :], ident[:, :])
                            s["sTr"].inc(i2)

                @block.scalar
                def _(sc):
                    for i in range(nf * npk):
                        sc.wait_ge(s["sTr"].h, tr0 + i + 1)
                        if o0 + i >= 2:
                            ost_.wait(sc, o0 + i - 2)
                        s["sPk"].inc(sc.copy(wsg[:, i % 2, :, :], ps_tr[i % 2][:, :]))

                @block.sync
                def _(sy):
                    for f in range(nf):
                        for pk in range(npk):
                            i = f * npk + pk
                            sy.wait_ge(s["sPk"].h, pk0 + i + 1)
                            ost_.issue(sy, lambda f=f, pk=pk, i=i: sy.dma_start(
                                out=o_st2.rearrange("(t p) n -> p t n", p=128)[
                                    :, pk * 4:pk * 4 + 4, f * 128:(f + 1) * 128],
                                in_=wsg[:, i % 2, :, :]))

        with nc.Block() as block:
            @block.sync
            def _(sy):
                ost_.wait_all(sy)



def emit_ffn0(nc, st, s, sd, csb, io, tag):
    xn_in, x_in, o_out = io["xn_in"], io["x_in"], io["o_out"]
    wuptile = io["wuptile"]
    wdntile = io["wdntile"]
    wdnpair = io["wdnpair"]
    bup, bdn = io["bup"], io["bdn"]
    ula, ulb, dla, dlb = io["ula"], io["ulb"], io["dla"], io["dlb"]
    wst, dst_s, rst, ost, xst = sd["A2"], sd["F3"], sd["C2"], sd["B2"], sd["H1"]
    wst2, dst2 = sd["E4"], sd["G3"]

    with contextlib.ExitStack() as pst:
        E = pst.enter_context
        xn = E(nc.sbuf_tensor(f"{tag}xn", [128, DMT, TC], BF16))
        x2 = E(nc.sbuf_tensor(f"{tag}x2", [128, HT, TC], BF16))
        wbuf = E(nc.sbuf_tensor(f"{tag}wbuf", [128, 2, DMT, 128], BF16))
        dbuf = E(nc.sbuf_tensor(f"{tag}dbuf", [128, 3, 2, 512], BF16))
        ulab = E(nc.sbuf_tensor(f"{tag}ulab", [128, DMT, 16], BF16))
        ulbb = E(nc.sbuf_tensor(f"{tag}ulbb", [16, HT * 128], BF16))
        dlab = E(nc.sbuf_tensor(f"{tag}dlab", [128, HT, 16], BF16))
        dlbb = E(nc.sbuf_tensor(f"{tag}dlbb", [16, DM], BF16))
        bupb = E(nc.sbuf_tensor(f"{tag}bupb", [128, HT], F32))
        bdnb = E(nc.sbuf_tensor(f"{tag}bdnb", [128, DMT], F32))
        u1 = E(nc.sbuf_tensor(f"{tag}u1", [16, TC], BF16))
        u2 = E(nc.sbuf_tensor(f"{tag}u2", [16, TC], BF16))
        og = E(nc.sbuf_tensor(f"{tag}og", [128, 2, TC], F32))
        out_sb = E(nc.sbuf_tensor(f"{tag}out_sb", [128, 2, TC], F32))
        rp_sb = E(nc.sbuf_tensor(f"{tag}rp_sb", [128, 2, TC], F32))

        l00 = s["sL"].n
        ll0 = s["sLl"].n
        u10 = s["sU1"].n
        u1e0 = s["sU1e"].n
        u20 = s["sU2"].n
        u2e0 = s["sU2e"].n
        x00 = xst.i
        o000 = ost.i

        with nc.Block() as block:
            @block.sync
            def _(sy):
                for f in range(HT):
                    s["sL"].inc(sy.dma_start(out=bupb[:, f:f + 1],
                                             in_=bup[f * 128:(f + 1) * 128, 0:1]), 16)
                for f in range(DMT):
                    s["sL"].inc(sy.dma_start(out=bdnb[:, f:f + 1],
                                             in_=bdn[f * 128:(f + 1) * 128, 0:1]), 16)
                sy.wait_ge(s["sL"].h, s["sL"].n)

            @block.gpsimd
            def _(g):
                s["sLl"].inc(g.dma_start(out=ulab[:, :, :],
                                         in_=ula.rearrange("(t p) r -> p t r", p=128)), 16)
                s["sLl"].inc(g.dma_start(out=dlab[:, :, :],
                                         in_=dla.rearrange("(t p) r -> p t r", p=128)), 16)
                s["sLl"].inc(g.dma_start(out=ulbb[:, :], in_=ulb), 16)
                s["sLl"].inc(g.dma_start(out=dlbb[:, :], in_=dlb), 16)
                g.wait_ge(s["sLl"].h, s["sLl"].n)

        for c in range(NCH):
            with nc.Block() as block, \
                    nc.psum_tensor(f"{tag}ps_u1{c}", [16, TC], F32) as ps_u1:
                @block.gpsimd
                def _(g):
                    xst.issue(g, lambda c=c: g.dma_start(
                        out=xn[:, :, :], in_=_fm(xn_in)[:, :, c * TC:(c + 1) * TC]))

                @block.tensor
                def _(t):
                    xst.wait(t, x00 + c)
                    for kt in range(DMT):
                        i2 = t.matmul(ps_u1[:, :], ulab[:, kt, :], xn[:, kt, :],
                                      start=(kt == 0), stop=(kt == DMT - 1))
                    s["sU1"].inc(i2)

                @block.scalar
                def _(sc):
                    sc.wait_ge(s["sU1"].h, s["sU1"].n)
                    s["sU1e"].inc(sc.activation(u1[:, :], ps_u1[:, :], AF.Identity))

            with contextlib.ExitStack() as bst:
                block = bst.enter_context(nc.Block())
                ps_up = [bst.enter_context(nc.psum_tensor(f"{tag}ps_up{c}_{j}", [128, TC], F32))
                         for j in range(4)]
                up0 = s["sUp"].n
                ue0 = s["sUe"].n
                w0 = wst.i
                w20 = wst2.i

                @block.gpsimd
                def _(g):
                    for f in range(0, HT, 2):
                        if f >= 2:
                            g.wait_ge(s["sUp"].h, up0 + f - 1)
                        wst.issue(g, lambda f=f: g.dma_start(
                            out=wbuf[:, 0, :, :], in_=wuptile(f)))

                @block.sync
                def _(sy):
                    for f in range(1, HT, 2):
                        if f >= 2:
                            sy.wait_ge(s["sUp"].h, up0 + f - 1)
                        wst2.issue(sy, lambda f=f: sy.dma_start(
                            out=wbuf[:, 1, :, :], in_=wuptile(f)))

                @block.tensor
                def _(t):
                    for f in range(HT):
                        if f % 2 == 0:
                            wst.wait(t, w0 + f // 2)
                        else:
                            wst2.wait(t, w20 + f // 2)
                        if f >= 4:
                            t.wait_ge(s["sUe"].h, ue0 + f - 3)
                        for kt in range(DMT):
                            t.matmul(ps_up[f % 4][:, :], wbuf[:, f % 2, kt, :],
                                     xn[:, kt, :], start=(kt == 0), stop=False)
                        i2 = t.matmul(ps_up[f % 4][:, :], ulbb[:, f * 128:(f + 1) * 128],
                                      u1[:, :], start=False, stop=True)
                        s["sUp"].inc(i2)

                @block.scalar
                def _(sc):
                    for f in range(HT):
                        sc.wait_ge(s["sUp"].h, up0 + f + 1)
                        s["sUe"].inc(sc.activation(x2[:, f, :], ps_up[f % 4][:, :],
                                                   AF.Relu, bias=bupb[:, f:f + 1]))

            with nc.Block() as block, \
                    nc.psum_tensor(f"{tag}ps_u2{c}", [16, TC], F32) as ps_u2:
                @block.tensor
                def _(t):
                    for kt in range(HT):
                        i2 = t.matmul(ps_u2[:, :], dlab[:, kt, :], x2[:, kt, :],
                                      start=(kt == 0), stop=(kt == HT - 1))
                    s["sU2"].inc(i2)

                @block.scalar
                def _(sc):
                    sc.wait_ge(s["sU2"].h, s["sU2"].n)
                    s["sU2e"].inc(sc.activation(u2[:, :], ps_u2[:, :], AF.Identity))

            with contextlib.ExitStack() as bst:
                block = bst.enter_context(nc.Block())
                ps_dn = [bst.enter_context(nc.psum_tensor(f"{tag}ps_dn{c}_{j}", [128, TC], F32))
                         for j in range(8)]
                dn0 = s["sDn"].n
                db0 = s["sDb"].n
                de0 = s["sDe"].n
                oa0 = s["sOa"].n
                d0 = dst_s.i
                d20 = dst2.i
                r0 = rst.i

                @block.sync
                def _(sy):
                    for i2 in range(4 * HT):
                        if i2 >= 3:
                            sy.wait_ge(s["sDb"].h, db0 + 2 * i2 - 4)
                        dst_s.issue(sy, lambda i2=i2: sy.dma_start(
                            out=dbuf[:, i2 % 3, :, :], in_=wdnpair(i2)))

                @block.gpsimd
                def _(g):
                    for f in range(DMT):
                        if f >= 2:
                            g.wait_ge(s["sOa"].h, oa0 + f - 1)
                        rst.issue(g, lambda f=f, c=c: g.dma_start(
                            out=rp_sb[:, f % 2, :],
                            in_=_fm(x_in)[:, f, c * TC:(c + 1) * TC]))
                        if f >= 1:
                            g.wait_ge(s["sOa"].h, oa0 + f)
                            ost.issue(g, lambda f=f, c=c: g.dma_start(
                                out=_fm(o_out)[:, f - 1, c * TC:(c + 1) * TC],
                                in_=out_sb[:, (f - 1) % 2, :]))
                    g.wait_ge(s["sOa"].h, oa0 + DMT)
                    ost.issue(g, lambda c=c: g.dma_start(
                        out=_fm(o_out)[:, DMT - 1, c * TC:(c + 1) * TC],
                        in_=out_sb[:, (DMT - 1) % 2, :]))

                @block.tensor
                def _(t):
                    for gq in range(8):
                        b4 = (gq % 2) * 4
                        if gq >= 2:
                            t.wait_ge(s["sDe"].h, de0 + (gq - 1) * 4)
                        for kt in range(HT):
                            i = gq * HT + kt
                            dst_s.wait(t, d0 + i // 2)
                            for d in range(4):
                                i2 = t.matmul(ps_dn[b4 + d][:, :],
                                              dbuf[:, (i // 2) % 3, i % 2,
                                                   d * 128:(d + 1) * 128],
                                              x2[:, kt, :], start=(kt == 0), stop=False)
                            s["sDb"].inc(i2)
                        t.wait_ge(s["sU2e"].h, u2e0 + c + 1)
                        for d in range(4):
                            gd = gq * 4 + d
                            s["sDn"].inc(t.matmul(
                                ps_dn[b4 + d][:, :], dlbb[:, gd * 128:(gd + 1) * 128],
                                u2[:, :], start=False, stop=True))

                @block.scalar
                def _(sc):
                    for gq in range(8):
                        for d in range(4):
                            f = gq * 4 + d
                            sc.wait_ge(s["sDn"].h, dn0 + gq * 4 + d + 1)
                            if f >= 2:
                                sc.wait_ge(s["sOa"].h, oa0 + f - 1)
                            s["sDe"].inc(sc.activation(og[:, f % 2, :],
                                                       ps_dn[(gq % 2) * 4 + d][:, :],
                                                       AF.Identity, bias=bdnb[:, f:f + 1]))

                @block.vector
                def _(v):
                    for f in range(DMT):
                        v.wait_ge(s["sDe"].h, de0 + f + 1)
                        rst.wait(v, r0 + f)
                        gi = c * DMT + f
                        if gi >= 2:
                            ost.wait(v, o000 + gi - 2)
                        s["sOa"].inc(v.tensor_add(out_sb[:, f % 2, :], og[:, f % 2, :],
                                                  rp_sb[:, f % 2, :]))

        with nc.Block() as block:
            @block.sync
            def _(sy):
                ost.wait_all(sy)



def emit_ffn1(nc, st, s, sd, csb, io, tag):
    xn_in, r1_in, a1_in, o_out = io["xn_in"], io["r1_in"], io["a1_in"], io["o_out"]
    up_in, dn_in = io["up_in"], io["dn_in"]
    permc = io["permc"]
    w_gate = io.get("w_gate")
    upst, dnst, r1st, a1st, ost = sd["F3"], sd["G3"], sd["C2"], sd["B2"], sd["A2"]
    r1v = r1_in.rearrange("r (c n) -> (r c) n", n=TC)
    a1v = a1_in.rearrange("r (c n) -> (r c) n", n=TC)
    ov = o_out.rearrange("r (c n) -> (r c) n", n=TC)
    NHH = NCH * 16

    with contextlib.ExitStack() as pst:
        E = pst.enter_context
        xp = E(nc.sbuf_tensor(f"{tag}xp", [128, DMT, T], BF16))
        yp = E(nc.sbuf_tensor(f"{tag}yp", [128, DMT, TC], F32))
        gbuf = E(nc.sbuf_tensor(f"{tag}gbuf", [128, 8, TC], BF16))
        upb = E(nc.sbuf_tensor(f"{tag}upb", [128, 2, 2, 1024], BF16))
        dnb = E(nc.sbuf_tensor(f"{tag}dnb", [128, 2, 8, 256], BF16))
        out_sb = E(nc.sbuf_tensor(f"{tag}out_sb", [128, 2, TC], F32))
        r1_sb = E(nc.sbuf_tensor(f"{tag}r1_sb", [128, 2, TC], F32))
        a1_sb = E(nc.sbuf_tensor(f"{tag}a1_sb", [128, 2, TC], F32))
        pidx = E(nc.sbuf_tensor(f"{tag}pidx", [128, DMT], I32))
        idxc = E(nc.sbuf_tensor(f"{tag}idxc", [128, 2, DMT], I32))

        pc0 = s["sPc"].n
        ptg0 = s["sPt"].n
        bc0 = s["sBc"].n
        o000 = ost.i

        with nc.Block() as block:
            @block.gpsimd
            def _(g):
                for gt in range(DMT):
                    s["sPc"].inc(g.dma_start(out=pidx[:, gt:gt + 1],
                                             in_=permc[gt * 128:(gt + 1) * 128, 0:1]), 16)
                g.wait_ge(s["sPc"].h, pc0 + 16 * DMT)
                for gt in range(DMT):
                    s["sPt"].inc(g.indirect_dma_start(
                        xp[:, gt, :], None,
                        xn_in, IndirectOffsetOnAxis(pidx[:, gt:gt + 1], 0)), 16)

            @block.vector
            def _(v):
                v.wait_ge(s["sPc"].h, pc0 + 16 * DMT)
                for c in range(NCH):
                    s["sBc"].inc(v.tensor_scalar(idxc[:, c, :], pidx[:, :],
                                                 2, c, ALU.mult, ALU.add))

        with contextlib.ExitStack() as bst:
            block = bst.enter_context(nc.Block())
            ps_up = [bst.enter_context(nc.psum_tensor(f"{tag}ps_hu{j}", [128, TC], F32))
                     for j in range(4)]
            ps_dn = [bst.enter_context(nc.psum_tensor(f"{tag}ps_hd{j}", [128, TC], F32))
                     for j in range(2)]
            up0 = s["sUp"].n
            ge0 = s["sGe"].n
            dn0 = s["sDn"].n
            ye0 = s["sYe"].n
            oa0 = s["sOa"].n
            u0 = upst.i
            d0 = dnst.i
            r0 = r1st.i
            a0 = a1st.i

            @block.sync
            def _(sy):
                if w_gate is not None:
                    sy.wait_ge(s["skn"].h, w_gate)
                for hh in range(NHH):
                    h = hh % 16
                    if hh >= 2:
                        sy.wait_ge(s["sUp"].h, up0 + (hh - 1) * 8)
                    upst.issue(sy, lambda h=h: sy.dma_start(
                        out=upb[:, h % 2, :, :],
                        in_=_fm(up_in)[:, 2 * h:2 * h + 2, :]))
                    if hh >= 2:
                        sy.wait_ge(s["sDn"].h, dn0 + (hh - 1) * 2)
                    dnst.issue(sy, lambda h=h: sy.dma_start(
                        out=dnb[:, h % 2, :, :],
                        in_=_fm(dn_in)[:, 8 * h:8 * h + 8, :]))

            @block.tensor
            def _(t):
                for hh in range(NHH):
                    c, h = divmod(hh, 16)
                    upst.wait(t, u0 + hh)
                    t.wait_ge(s["sPt"].h, ptg0 + 16 * (2 * h + 2))
                    for mt in range(8):
                        if hh * 8 + mt >= 4:
                            t.wait_ge(s["sGe"].h, ge0 + hh * 8 + mt - 3)
                        for kt in range(2):
                            i2 = t.matmul(ps_up[mt % 4][:, :],
                                          upb[:, h % 2, kt, mt * 128:(mt + 1) * 128],
                                          xp[:, 2 * h + kt, c * TC:(c + 1) * TC],
                                          start=(kt == 0), stop=(kt == 1))
                        s["sUp"].inc(i2)
                    dnst.wait(t, d0 + hh)
                    for dt in range(2):
                        if hh >= 1:
                            t.wait_ge(s["sYe"].h, ye0 + (hh - 1) * 2 + dt + 1)
                        for mt in range(8):
                            t.wait_ge(s["sGe"].h, ge0 + hh * 8 + mt + 1)
                            i2 = t.matmul(ps_dn[dt][:, :],
                                          dnb[:, h % 2, mt, dt * 128:(dt + 1) * 128],
                                          gbuf[:, mt, :],
                                          start=(mt == 0), stop=(mt == 7))
                        s["sDn"].inc(i2)

            @block.scalar
            def _(sc):
                for hh in range(NHH):
                    c, h = divmod(hh, 16)
                    for mt in range(8):
                        sc.wait_ge(s["sUp"].h, up0 + hh * 8 + mt + 1)
                        if hh >= 1:
                            sc.wait_ge(s["sDn"].h, dn0 + 2 * hh)
                        s["sGe"].inc(sc.activation(gbuf[:, mt, :], ps_up[mt % 4][:, :],
                                                   AF.Gelu))
                    for dt in range(2):
                        f = 2 * h + dt
                        sc.wait_ge(s["sDn"].h, dn0 + hh * 2 + dt + 1)
                        if c == 1:
                            sc.wait_ge(s["sOa"].h, oa0 + f + 1)
                        s["sYe"].inc(sc.activation(yp[:, f, :], ps_dn[dt][:, :],
                                                   AF.Identity))

            @block.vector
            def _(v):
                ch = s["chV"]
                for c in range(NCH):
                    for f in range(DMT):
                        gi = c * DMT + f
                        v.wait_ge(s["sYe"].h, ye0 + c * DMT + f + 1)
                        r1st.wait(v, r0 + gi)
                        a1st.wait(v, a0 + gi)
                        if gi >= 2:
                            ost.wait(v, o000 + gi - 2)
                        ch.inc(v.tensor_add(out_sb[:, f % 2, :], yp[:, f, :],
                                            r1_sb[:, f % 2, :]))
                        v.wait_ge(ch.h, ch.n)
                        s["sOa"].inc(v.tensor_add(out_sb[:, f % 2, :], out_sb[:, f % 2, :],
                                                  a1_sb[:, f % 2, :]))

            @block.gpsimd
            def _(g):
                g.wait_ge(s["sBc"].h, bc0 + NCH)
                for c in range(NCH):
                    for f in range(DMT):
                        gi = c * DMT + f
                        if gi >= 2:
                            g.wait_ge(s["sOa"].h, oa0 + gi - 1)
                        r1st.issue(g, lambda f=f, c=c: g.indirect_dma_start(
                            r1_sb[:, f % 2, :], None,
                            r1v, IndirectOffsetOnAxis(idxc[:, c, f:f + 1], 0)))
                        a1st.issue(g, lambda f=f, c=c: g.indirect_dma_start(
                            a1_sb[:, f % 2, :], None,
                            a1v, IndirectOffsetOnAxis(idxc[:, c, f:f + 1], 0)))
                        if gi >= 1:
                            cc, ff = divmod(gi - 1, DMT)
                            g.wait_ge(s["sOa"].h, oa0 + gi)
                            ost.issue(g, lambda cc=cc, ff=ff, gi=gi: g.indirect_dma_start(
                                ov, IndirectOffsetOnAxis(idxc[:, cc, ff:ff + 1], 0),
                                out_sb[:, (gi - 1) % 2, :], None))
                g.wait_ge(s["sOa"].h, oa0 + NCH * DMT)
                ost.issue(g, lambda: g.indirect_dma_start(
                    ov, IndirectOffsetOnAxis(idxc[:, NCH - 1, DMT - 1:DMT], 0),
                    out_sb[:, (NCH * DMT - 1) % 2, :], None))

        with nc.Block() as block:
            @block.sync
            def _(sy):
                ost.wait_all(sy)


def emit_ag(nc, s, pairs, groups, wait=True):
    with nc.Block() as block:
        @block.gpsimd
        def _(g):
            for i, o in pairs:
                s["sCC"].inc(g.collective_compute(
                    "AllGather", ALU.bypass, replica_groups=groups,
                    ins=[i], outs=[o]), 1)
            if wait:
                g.wait_ge(s["sCC"].h, s["sCC"].n)
    return s["sCC"].n


def emit_ag_wait(nc, s, upto):
    with nc.Block() as block:
        @block.gpsimd
        def _(g):
            g.wait_ge(s["sCC"].h, upto)



G8 = [list(range(8))]
G4 = [[0, 1, 2, 3], [4, 5, 6, 7]]


def build_fused():
    nc = bass.Bass("TRN2", target_bir_lowering=False)
    ap = lambda *a, **k: nc.dram_tensor(*a, **k).ap()
    x_in = ap("x_in", [DM, T], F32, kind="ExternalInput")
    wqkv0 = ap("wqkv0", [DM, 3072], F32, kind="ExternalInput")
    bq0 = ap("bq0", [3072, 1], F32, kind="ExternalInput")
    wo0 = ap("wo0", [1024, DM], F32, kind="ExternalInput")
    bo0 = ap("bo0", [DM, 1], F32, kind="ExternalInput")
    upc = ap("upc", [RUP, DM], I32, kind="ExternalInput")
    upa = ap("upa", [RUP, 64], F32, kind="ExternalInput")
    dnc = ap("dnc", [DNR, 11008], I32, kind="ExternalInput")
    dna = ap("dna", [DNR, 172], F32, kind="ExternalInput")
    bup = ap("bup", [11008, 1], F32, kind="ExternalInput")
    bdn = ap("bdn", [DM, 1], F32, kind="ExternalInput")
    ula = ap("ula", [DM, 16], F32, kind="ExternalInput")
    ulb = ap("ulb", [16, 11008], F32, kind="ExternalInput")
    dla = ap("dla", [11008, 16], F32, kind="ExternalInput")
    dlb = ap("dlb", [16, DM], F32, kind="ExternalInput")
    wq1s = ap("wq1s", [768, DM], F32, kind="ExternalInput")
    bq1 = ap("bq1", [6144, 1], F32, kind="ExternalInput")
    wo1s = ap("wo1s", [512, 2048], F32, kind="ExternalInput")
    bo1 = ap("bo1", [DM, 1], F32, kind="ExternalInput")
    up1 = ap("up1", [DM, 1024], F32, kind="ExternalInput")
    dn1 = ap("dn1", [16384, 256], F32, kind="ExternalInput")
    permr = ap("permr", [1, DM], I32, kind="ExternalInput")
    permc = ap("permc", [DM, 1], I32, kind="ExternalInput")
    c_jrow = ap("c_jrow", [1, DM], F32, kind="ExternalInput")
    mk_in = ap("mk_in", [NCH, KTOT, 128, TC], BF16, kind="ExternalInput")
    c_dqd = ap("c_dqd", [16 * 128, 256], BF16, kind="ExternalInput")
    c_ident = ap("c_ident", [128, 128], BF16, kind="ExternalInput")
    c_ones_bf = ap("c_ones_bf", [128, 1], BF16, kind="ExternalInput")
    c_ones_bfr = ap("c_ones_bfr", [1, 128], BF16, kind="ExternalInput")
    c_ones_f = ap("c_ones_f", [1, 128], F32, kind="ExternalInput")
    o_out = ap("o_out", [DM, T], F32, kind="ExternalOutput")
    dq_up_st = ap("dq_up_st", [DM, RUP], BF16)
    dq_dn_st = ap("dq_dn_st", [11008, DNR], BF16)
    wup_ag = ap("wup_ag", [8 * DM, RUP], BF16, addr_space="Shared")
    wdn_ag = ap("wdn_ag", [8 * 11008, DNR], BF16, addr_space="Shared")
    wq_st = ap("wq_st", [DM, 768], BF16)
    wq_ag = ap("wq_ag", [8 * DM, 768], BF16, addr_space="Shared")
    wo_st = ap("wo_st", [2048, 512], BF16)
    wo_ag = ap("wo_ag", [8 * 2048, 512], BF16, addr_space="Shared")
    xn0 = ap("xn0", [DM, T], BF16)
    up1_bf = ap("up1_bf", [DM, 1024], BF16)
    dn1_bf = ap("dn1_bf", [16384, 256], BF16)
    q0 = ap("q0", [1024, T], BF16)
    k0_st = ap("k0_st", [1024, T], BF16)
    v0_st = ap("v0_st", [T, 1024], BF16)
    qn0 = ap("qn0", [1, T], F32)
    kn0_st = ap("kn0_st", [1, T], F32)
    k0_ag = ap("k0_ag", [4 * 1024, T], BF16)
    v0_ag = ap("v0_ag", [4 * T, 1024], BF16)
    kn0_ag = ap("kn0_ag", [4, T], F32)
    v0fm = ap("v0fm", [1024, T], BF16)
    residp = ap("residp", [DM, T], F32)
    resid1 = ap("resid1", [DM, T], F32)
    xn1 = ap("xn1", [DM, T], BF16)
    q1 = ap("q1", [2048, T], BF16)
    k1_st = ap("k1_st", [2048, T], BF16)
    v1_st = ap("v1_st", [T, 2048], BF16)
    qn1 = ap("qn1", [1, 8 * T], F32)
    kn1_st = ap("kn1_st", [1, 8 * T], F32)
    k1_ag = ap("k1_ag", [4 * 2048, T], BF16)
    v1_ag = ap("v1_ag", [4 * T, 2048], BF16)
    kn1_ag = ap("kn1_ag", [4, 8 * T], F32)
    v1fm = ap("v1fm", [2048, T], BF16)
    attn1 = ap("attn1", [DM, T], F32)

    with contextlib.ExitStack() as st:
        E = st.enter_context
        csb = dict(
            ident=E(nc.sbuf_tensor("g_ident", [128, 128], BF16)),
            ones_bf=E(nc.sbuf_tensor("g_ones_bf", [128, 1], BF16)),
            ones_bfr=E(nc.sbuf_tensor("g_ones_bfr", [1, 128], BF16)),
            ones_f=E(nc.sbuf_tensor("g_ones_f", [1, 128], F32)),
        )
        s = _sems(nc, st, SEM_NAMES)
        sd = _mkstreams(nc, st)

        with nc.Block() as block:
            @block.sync
            def _(sy):
                s["sI"].inc(sy.dma_start(out=csb["ident"][:, :], in_=c_ident), 16)
                s["sI"].inc(sy.dma_start(out=csb["ones_bf"][:, :], in_=c_ones_bf), 16)
                s["sI"].inc(sy.dma_start(out=csb["ones_bfr"][:, :], in_=c_ones_bfr), 16)
                s["sI"].inc(sy.dma_start(out=csb["ones_f"][:, :], in_=c_ones_f), 16)
                sy.wait_ge(s["sI"].h, s["sI"].n)

        csba = {k: v[:, :] for k, v in csb.items()}

        emit_dequant(nc, st, s, sd, csba,
                     dict(upc=upc, upa=upa, dnc=dnc, dna=dna, dqd=c_dqd,
                          o_up=dq_up_st, o_dn=dq_dn_st), "dq")
        emit_wtprep(nc, st, s, sd, csba,
                    dict(jobs=[(wq1s, wq_st), (wo1s, wo_st)]), "wt")
        agw = emit_ag(nc, s, [(dq_up_st, wup_ag), (dq_dn_st, wdn_ag),
                              (wq_st, wq_ag), (wo_st, wo_ag)], G8, wait=False)

        emit_pre(nc, st, s, sd, csba,
                 dict(xin=x_in, bvec=bq0,
                      wtile=lambda f: _fm(wqkv0)[:, :, f * 128:(f + 1) * 128],
                      o_xn=xn0, o_q=q0, o_k=k0_st, o_v=v0_st,
                      o_qn=qn0, o_kn=kn0_st, v_fm=v0fm), 0, "p0")
        ag0 = emit_ag(nc, s, [(k0_st, k0_ag), (v0_st, v0_ag), (kn0_st, kn0_ag)], G4,
                      wait=False)
        emit_ag_wait(nc, s, agw)

        wupv = wup_ag.rearrange("(g t p) n -> p (g t) n", g=8, p=128)
        wdnv = wdn_ag.rearrange("(g t p) n -> p (g t) n", g=8, p=128)
        emit_ffn0(nc, st, s, sd, csba,
                  dict(xn_in=xn0, x_in=x_in, o_out=residp,
                       wuptile=lambda ht: wupv[:, (ht // 11) * 32:(ht // 11) * 32 + 32,
                                               (ht % 11) * 128:(ht % 11) * 128 + 128],
                       wdntile=lambda gq, kt: wdnv[:, gq * HT + kt, :],
                       wdnpair=lambda i2: wdnv[:, 2 * i2:2 * i2 + 2, :],
                       bup=bup, bdn=bdn, ula=ula, ulb=ulb, dla=dla, dlb=dlb), "f0")

        emit_ag_wait(nc, s, ag0)
        k0v = k0_ag.rearrange("(g d p) t -> p (g d) t", g=4, p=128)
        emit_attn(nc, st, s, sd, csba,
                  dict(q_in=q0, v_in=v0_ag, qn_in=qn0,
                       ktile=lambda kt: k0v[:, (kt // 8) * 8:(kt // 8) * 8 + 8,
                                            (kt % 8) * 128:(kt % 8) * 128 + 128],
                       kn_loads=lambda h: [
                           ((lambda knb, g=g: knb[0:1, g * T:(g + 1) * T]),
                            kn0_ag[g:g + 1, :]) for g in range(4)],
                       wotile=lambda f: _fm(wo0)[:, :, f * 128:(f + 1) * 128],
                       bo_in=bo0, rp_in=residp, o_out=resid1), 0, "a0")

        wq1v = wq_ag.rearrange("(g t p) n -> p (g t) n", g=8, p=128)
        emit_pre(nc, st, s, sd, csba,
                 dict(xin=resid1, bvec=bq1,
                      wtile=lambda f: wq1v[:, (f // 6) * 32:(f // 6) * 32 + 32,
                                           (f % 6) * 128:(f % 6) * 128 + 128],
                      o_xn=xn1, o_q=q1, o_k=k1_st, o_v=v1_st,
                      o_qn=qn1, o_kn=kn1_st, v_fm=v1fm,
                      ag_early=((k1_st, k1_ag), (v1_st, v1_ag), G4),
                      ag_mid=([(kn1_st, kn1_ag), (v1_st, v1_ag)], G4)), 1, "p1")
        ag_k1 = s["sCC"].n - 2
        ag_kn1 = s["sCC"].n - 1
        ag_v1 = s["sCC"].n

        k1v = k1_ag.rearrange("(g d p) t -> p (g d) t", g=4, p=128)
        wo1v = wo_ag.rearrange("(g t p) n -> p (g t) n", g=8, p=128)
        emit_attn(nc, st, s, sd, csba,
                  dict(q_in=q1, v_in=v1_ag, qn_in=qn1,
                       k_loads=lambda h: [
                           ((lambda kf, g=g: kf[:, :, g * 1024:(g + 1) * 1024]),
                            k1v[:, g * 16 + 2 * h:g * 16 + 2 * h + 2, :]) for g in range(4)],
                       kn_loads=lambda h: [
                           ((lambda knb, g=g: knb[0:1, g * T:(g + 1) * T]),
                            kn1_ag[g:g + 1, h * T:(h + 1) * T]) for g in range(4)],
                       wotile=lambda f: wo1v[:, (f // 4) * 16:(f // 4) * 16 + 16,
                                             (f % 4) * 128:(f % 4) * 128 + 128],
                       bo_in=bo1, mk_in=mk_in, o_out=attn1,
                       ag_k=ag_k1, ag_v=ag_v1, ag_kn=ag_kn1,
                       post_kn=lambda g: (
                           s["skn"].inc(g.dma_start(out=up1_bf, in_=up1), 16),
                           s["skn"].inc(g.dma_start(out=dn1_bf, in_=dn1), 16))), 1, "a1")
        wprecast_t = s["skn"].n

        emit_ffn1(nc, st, s, sd, csba,
                  dict(xn_in=xn1, r1_in=resid1, a1_in=attn1, o_out=o_out,
                       up_in=up1_bf, dn_in=dn1_bf, permc=permc,
                       w_gate=wprecast_t), "f1")
    return nc



def _cn(a, dt=None):
    a = np.ascontiguousarray(a)
    return a if dt is None else a.astype(dt)


def _make_masks(q0g):
    m = np.empty((NCH, KTOT, 128, TC), dtype=np.float32)
    for qc in range(NCH):
        tglob = q0g + qc * TC + np.arange(TC)[None, :]
        for kt in range(KTOT):
            fglob = kt * 128 + np.arange(128)[:, None]
            m[qc, kt] = np.where(tglob >= fglob, 1e30, MASKVAL)
    return m.astype(BF)


def kernel(L0_attn_x, L0_attn_Wqkv, L0_attn_bqkv, L0_attn_Wo, L0_attn_bo,
           L0_ffn_w_up_codes, L0_ffn_w_up_absmax, L0_ffn_b_up,
           L0_ffn_w_up_lora_a, L0_ffn_w_up_lora_b,
           L0_ffn_w_down_codes, L0_ffn_w_down_absmax, L0_ffn_b_down,
           L0_ffn_w_down_lora_a, L0_ffn_w_down_lora_b,
           L1_attn_w_qkv, L1_attn_b_qkv, L1_attn_w_o, L1_attn_b_o,
           L1_ffn_up, L1_ffn_down, L1_ffn_perm):
    cst = _consts_np()
    x = np.asarray(L0_attn_x, np.float32)
    upc = np.asarray(L0_ffn_w_up_codes, np.int32).reshape(11008, DM)
    upa = np.asarray(L0_ffn_w_up_absmax, np.float32).reshape(11008, 64)
    dnc = np.asarray(L0_ffn_w_down_codes, np.int32).reshape(DM, 11008)
    dna = np.asarray(L0_ffn_w_down_absmax, np.float32).reshape(DM, 172)
    upc_p = np.zeros((NCORES * RUP, DM), np.int32)
    upc_p[:11008] = upc
    upa_p = np.zeros((NCORES * RUP, 64), np.float32)
    upa_p[:11008] = upa
    wqkv0 = _cn(L0_attn_Wqkv, np.float32)
    wq1 = np.asarray(L1_attn_w_qkv, np.float32)
    wo1 = np.asarray(L1_attn_w_o, np.float32)
    perm = np.asarray(L1_ffn_perm, np.int32)
    jrow = np.arange(DM, dtype=np.float32).reshape(1, -1)

    in_maps = []
    for c in range(NCORES):
        bi, sl = divmod(c, 4)
        in_maps.append(dict(
            x_in=_cn(x[bi, sl * T:(sl + 1) * T, :].T, np.float32),
            wqkv0=wqkv0,
            bq0=_cn(np.asarray(L0_attn_bqkv, np.float32).reshape(-1, 1)),
            wo0=_cn(L0_attn_Wo, np.float32),
            bo0=_cn(np.asarray(L0_attn_bo, np.float32).reshape(-1, 1)),
            upc=_cn(upc_p[c * RUP:(c + 1) * RUP]),
            upa=_cn(upa_p[c * RUP:(c + 1) * RUP]),
            dnc=_cn(dnc[c * DNR:(c + 1) * DNR]),
            dna=_cn(dna[c * DNR:(c + 1) * DNR]),
            bup=_cn(np.asarray(L0_ffn_b_up, np.float32).reshape(-1, 1)),
            bdn=_cn(np.asarray(L0_ffn_b_down, np.float32).reshape(-1, 1)),
            ula=_cn(L0_ffn_w_up_lora_a, np.float32),
            ulb=_cn(L0_ffn_w_up_lora_b, np.float32),
            dla=_cn(L0_ffn_w_down_lora_a, np.float32),
            dlb=_cn(L0_ffn_w_down_lora_b, np.float32),
            wq1s=_cn(wq1[c * 768:(c + 1) * 768]),
            bq1=_cn(np.asarray(L1_attn_b_qkv, np.float32).reshape(-1, 1)),
            wo1s=_cn(wo1[c * 512:(c + 1) * 512]),
            bo1=_cn(np.asarray(L1_attn_b_o, np.float32).reshape(-1, 1)),
            up1=_cn(np.asarray(L1_ffn_up, np.float32).reshape(DM, 1024)),
            dn1=_cn(np.asarray(L1_ffn_down, np.float32).reshape(16384, 256)),
            permr=_cn(perm.reshape(1, -1)),
            permc=_cn(perm.reshape(-1, 1)),
            c_jrow=jrow,
            mk_in=_make_masks(sl * T),
            **cst))

    nc = build_fused()
    res = run_bass_kernel_spmd(nc, in_maps, core_ids=list(range(NCORES))).results
    out = np.empty((2, 4096, DM), np.float32)
    for c in range(NCORES):
        bi, sl = divmod(c, 4)
        out[bi, sl * T:(sl + 1) * T, :] = np.asarray(res[c]["o_out"]).T
    return out
